# version 16 (final)
# baseline (speedup 1.0000x reference)
from contextlib import ExitStack

import numpy as np
import ml_dtypes

import concourse.bass as bass
import concourse.mybir as mybir
from concourse.bass_utils import run_bass_kernel_spmd

F32 = mybir.dt.float32
BF16 = mybir.dt.bfloat16
I32 = mybir.dt.int32
AF = mybir.ActivationFunctionType
ALU = mybir.AluOpType
AX = mybir.AxisListType

D = 1024
NH = 16
HD = 64
S = 4096
NT = 8
TU = 4
NU = 32
E = 8
H2 = 2048
NEG = -1.0e30
MINE = [[0, 3, 4, 7], [1, 2, 5, 6]]


class _Eng:

    def __init__(self, nc, eng, name):
        self.e = eng
        self.name = name
        self.sem = nc.alloc_semaphore("pg_" + name)
        self.n = 0
        self.waited = {}
        self.is_dma = False

    def mark(self, instr):
        instr.then_inc(self.sem, 1)
        self.n += 1
        return instr

    def wait_marks(self, prod, lo, hi):
        if (prod is self and not prod.is_dma) or hi <= 0:
            return
        assert hi <= prod.n, (
            f"{self.name} waits {prod.name} future mark {hi} > {prod.n}")
        prod._emit_wait(self, lo, hi)

    def _emit_wait(self, waiter, lo, hi):
        key = self.name
        if waiter.waited.get(key, 0) >= hi:
            return
        waiter.e.wait_ge(self.sem, hi)
        waiter.waited[key] = hi

    def selfbar(self):
        if self.n > 0 and getattr(self, "last_sb", -1) < self.n:
            self.e.wait_ge(self.sem, self.n)
            self.last_sb = self.n

    def wstate(self):
        return self.n, dict(self.waited), getattr(self, "last_sb", -1)

    def wrestore(self, st):
        self.n, self.waited = st[0], dict(st[1])
        self.last_sb = st[2]


class _DmaEng:

    K = 8

    def __init__(self, nc, eng, name):
        self.e = eng
        self.name = name
        self.sems = [nc.alloc_semaphore(f"dq_{name}{i}") for i in range(self.K)]
        self.cnt = [0] * self.K
        self.n = 0
        self.waited = {}
        self.is_dma = True

    def mark(self, ins):
        k = self.n % self.K
        if self.cnt[k] > 0:
            ins._wait_ge(self.sems[k], self.cnt[k] * 16)
        ins.then_inc(self.sems[k], 16)
        self.cnt[k] += 1
        self.n += 1
        return ins

    def wait_marks(self, prod, lo, hi):
        if (prod is self and not prod.is_dma) or hi <= 0:
            return
        assert hi <= prod.n, (
            f"{self.name} waits {prod.name} future mark {hi} > {prod.n}")
        prod._emit_wait(self, lo, hi)

    def _emit_wait(self, waiter, lo, hi):
        want = {}
        for m in range(lo, hi + 1):
            k = (m - 1) % self.K
            c = (m - 1) // self.K + 1
            want[k] = max(want.get(k, 0), c)
        for k, c in want.items():
            key = (self.name, k)
            if waiter.waited.get(key, 0) >= c:
                continue
            waiter.e.wait_ge(self.sems[k], c * 16)
            waiter.waited[key] = c

    def drain(self):
        for k in range(self.K):
            if self.cnt[k] > 0:
                self.e.wait_ge(self.sems[k], self.cnt[k] * 16)

    def wstate(self):
        return self.n, dict(self.waited), list(self.cnt)

    def wrestore(self, st):
        self.n, self.waited, self.cnt = st[0], dict(st[1]), list(st[2])


class _SB:
    def __init__(self, engs):
        self.w = {}
        self.r = {}
        self.engs = engs

    def rd(self, eng, *keys):
        for k in keys:
            wr = self.w.get(k)
            if wr is not None:
                if wr[0] is eng and not eng.is_dma:
                    assert wr[2] <= eng.n, (k, wr, eng.name, eng.n)
                    if wr[2] >= eng.n - 2:
                        eng.selfbar()
                else:
                    eng.wait_marks(wr[0], wr[1], wr[2])
            self.r.setdefault(k, []).append((eng, eng.n + 1))

    def wr(self, eng, *keys):
        for k in keys:
            wr = self.w.get(k)
            if wr is not None:
                eng.wait_marks(wr[0], wr[1], wr[2])
            for re, rn in self.r.get(k, ()):
                eng.wait_marks(re, rn, rn)
            self.r[k] = []
            self.w[k] = (eng, eng.n + 1, eng.n + 1)

    def bind_range(self, eng, lo_n, *keys):
        for k in keys:
            self.w[k] = (eng, lo_n + 1, eng.n)

    def barrier(self):
        for a in self.engs:
            for b in self.engs:
                a.wait_marks(b, 1, b.n)

    def state(self):
        return (dict(self.w), {k: list(v) for k, v in self.r.items()},
                {e.name: e.n for e in self.engs},
                {e.name: e.wstate() for e in self.engs})

    def restore(self, st):
        self.w = dict(st[0])
        self.r = {k: list(v) for k, v in st[1].items()}
        for e in self.engs:
            e.wrestore(st[3][e.name])


def _rope_tables():
    invf = (1.0 / 10000.0 ** (np.arange(0, 16, 2) / 16)).astype(np.float64)
    fr = np.outer(np.arange(S, dtype=np.float64), invf)
    ct = np.cos(fr).astype(np.float32).reshape(NU, 128, 8).transpose(1, 0, 2)
    st = np.sin(fr).astype(np.float32).reshape(NU, 128, 8).transpose(1, 0, 2)
    return np.ascontiguousarray(ct), np.ascontiguousarray(st)


def _band_masks():
    i = np.arange(128)[:, None]
    kw = np.arange(143)[None, :]
    band = np.where((kw >= i) & (kw <= i + 15), 0.0, NEG).astype(np.float32)
    band0 = np.where(kw >= 15, band, NEG).astype(np.float32)
    return band, band0


def _diag_mask():
    i = np.arange(128)[:, None]
    k = np.arange(128)[None, :]
    return np.where(k <= i, 0.0, NEG).astype(np.float32)


def build(dump=None):
    nc = bass.Bass("TRN2", target_bir_lowering=False,
                   detect_race_conditions=False)

    xf_h = nc.dram_tensor("xf", [S, D], F32, kind="ExternalInput")
    xs_h = nc.dram_tensor("xs", [4, 512, D], F32, kind="ExternalInput")
    jsel_h = nc.dram_tensor("jsel", [1, 1], I32, kind="ExternalInput")
    wqkv_h = nc.dram_tensor("wqkv", [3 * D, D], F32, kind="ExternalInput")
    bqkv_h = nc.dram_tensor("bqkv", [3 * D], F32, kind="ExternalInput")
    wqa_h = nc.dram_tensor("wqa", [D, D], F32, kind="ExternalInput")
    bqa_h = nc.dram_tensor("bqa", [D], F32, kind="ExternalInput")
    wka_h = nc.dram_tensor("wka", [D, D], F32, kind="ExternalInput")
    bka_h = nc.dram_tensor("bka", [D], F32, kind="ExternalInput")
    wd_h = nc.dram_tensor("wd", [D, D], F32, kind="ExternalInput")
    bd_h = nc.dram_tensor("bd", [D], F32, kind="ExternalInput")
    rw_h = nc.dram_tensor("rw", [D, E], F32, kind="ExternalInput")
    rb_h = nc.dram_tensor("rb", [E], F32, kind="ExternalInput")
    sw1_h = nc.dram_tensor("sw1", [D, H2], F32, kind="ExternalInput")
    sb1_h = nc.dram_tensor("sb1", [H2], F32, kind="ExternalInput")
    sw2_h = nc.dram_tensor("sw2", [H2, D], F32, kind="ExternalInput")
    sb2_h = nc.dram_tensor("sb2", [D], F32, kind="ExternalInput")
    ew1_h = nc.dram_tensor("ew1", [E, D, H2], F32, kind="ExternalInput")
    eb1_h = nc.dram_tensor("eb1", [E, H2], F32, kind="ExternalInput")
    ew2_h = nc.dram_tensor("ew2", [E, H2, D], F32, kind="ExternalInput")
    eb2_h = nc.dram_tensor("eb2", [E, D], F32, kind="ExternalInput")
    out_h = nc.dram_tensor("out", [4, 128, TU, D], F32, kind="ExternalOutput")

    wqkvT_c = nc.dram_tensor("wqkvT_c", [128, 8, 3 * D], BF16, kind="Internal")
    wqaT_c = nc.dram_tensor("wqaT_c", [128, 8, D], BF16, kind="Internal")
    wkaT_c = nc.dram_tensor("wkaT_c", [128, 8, D], BF16, kind="Internal")
    wdT_c = nc.dram_tensor("wdT_c", [128, 8, D], BF16, kind="Internal")
    spill_kind = "ExternalOutput" if dump == "B" else "Internal"
    if dump == "C1":
        xn2a_d = nc.dram_tensor("xn2a_d", [128, 8, 2048], BF16,
                                kind="ExternalOutput")
        wall_d = nc.dram_tensor("wall_d", [128, 16, E], F32,
                                kind="ExternalOutput")
        mix_d = nc.dram_tensor("mix_d", [128, 16, D], F32,
                               kind="ExternalOutput")
        wt9_d = nc.dram_tensor("wt9_d", [9, 2048], BF16,
                               kind="ExternalOutput")
    x2_sp = nc.dram_tensor("x2_sp", [4, 128, TU, D], F32, kind=spill_kind)
    xn2_sp = nc.dram_tensor("xn2_sp", [4, 128, 8, 512], BF16, kind=spill_kind)
    if dump == "A":
        atm_d = nc.dram_tensor("atm_d", [128, NU, NH, 65], BF16,
                               kind="ExternalOutput")
        q_d = nc.dram_tensor("q_d", [128, TU, D], BF16, kind="ExternalOutput")
        k_d = nc.dram_tensor("k_d", [128, TU, D], BF16, kind="ExternalOutput")
        v_d = nc.dram_tensor("v_d", [128, TU, D], BF16, kind="ExternalOutput")
        pb_d = nc.dram_tensor("pb_d", [128, NH, 256], BF16,
                              kind="ExternalOutput")
        ws_d = nc.dram_tensor("ws_d", [128, NH], F32, kind="ExternalOutput")
        xnf_d = nc.dram_tensor("xnf_d", [128, 8, 512], BF16,
                               kind="ExternalOutput")
        st_d = nc.dram_tensor("st_d", [128, 16], F32, kind="ExternalOutput")
        sq_d = nc.dram_tensor("sq_d", [128, D], F32, kind="ExternalOutput")
        wq_d = nc.dram_tensor("wq_d", [128, 8, 3 * D], BF16,
                              kind="ExternalOutput")

    ident_h = nc.inline_tensor(np.eye(128, dtype=ml_dtypes.bfloat16), "identc")
    band_np, band0_np = _band_masks()
    band_h = nc.inline_tensor(band_np, "bandc")
    band0_h = nc.inline_tensor(band0_np, "band0c")
    diag_h = nc.inline_tensor(
        np.ascontiguousarray(_diag_mask().T).astype(ml_dtypes.bfloat16),
        "diagc")
    posug_np = (np.arange(128)[:, None] +
                128.0 * np.arange(NU)[None, :]).astype(np.float32)
    posug_h = nc.inline_tensor(posug_np, "posugc")
    i8_h = nc.inline_tensor(
        np.tile((np.arange(8) * 0.125).astype(np.float32), (128, 1)),
        "i8c")

    T = _Eng(nc, nc.tensor, "T")
    V = _Eng(nc, nc.vector, "V")
    SC = _Eng(nc, nc.scalar, "SC")
    SY = _DmaEng(nc, nc.sync, "SY")
    GP = _DmaEng(nc, nc.gpsimd, "GP")
    ALLE = (T, V, SC, SY, GP)
    sb = _SB(ALLE)

    st_ctx = ExitStack()
    sbuf = lambda n, s, d: st_ctx.enter_context(nc.sbuf_tensor(n, s, d))

    ident = sbuf("ident", [128, 128], BF16)
    ones_bf = sbuf("ones_bf", [1, 128], BF16)
    jsel_sb = sbuf("jsel_sb", [1, 1], I32)
    stats = sbuf("stats", [128, 16], F32)
    sq_sc = sbuf("sq_sc", [128, D], F32)
    epsc = sbuf("epsc", [128, 1], F32)

    sb.wr(SY, "consts")
    _n0 = SY.n
    SY.mark(nc.sync.dma_start(ident[:, :], ident_h[:, :]))
    SY.mark(nc.sync.dma_start(jsel_sb[:, :], jsel_h[:, :]))
    sb.bind_range(SY, _n0, "consts")
    sb.wr(V, "ones")
    nc.vector.memset(epsc[:, :], 1e-5)
    V.mark(nc.vector.memset(ones_bf[:, :], 1.0))
    for e in ALLE:
        sb.rd(e, "consts")
    jreg = nc.values_load(jsel_sb[0:1, 0:1],
                          engines=[e.e.engine for e in ALLE],
                          min_val=0, max_val=1,
                          skip_runtime_bounds_check=True)

    def layer_norm_batch(x_aps, out_aps, skey):
        n = len(x_aps)
        negm = lambda u: stats[:, 2 * u:2 * u + 1]
        rstd = lambda u: stats[:, 2 * u + 1:2 * u + 2]
        ssum = lambda u: stats[:, 8 + u:9 + u]
        stdv = lambda u: stats[:, 12 + u:13 + u]
        for u in range(n):
            sb.rd(V, skey)
            sb.wr(V, ("st_a", u))
            V.mark(nc.vector.reduce_sum(ssum(u), x_aps[u], axis=AX.X))
        V.selfbar()
        for u in range(n):
            sb.rd(V, ("st_a", u))
            sb.wr(V, ("st_b", u))
            V.mark(nc.vector.tensor_scalar_mul(negm(u), ssum(u), -1.0 / D))
        for u in range(n):
            sb.rd(SC, skey, ("st_b", u))
            sb.wr(SC, "sq_sc")
            SC.mark(nc.scalar.activation(sq_sc[:, :], x_aps[u], AF.Square,
                                         bias=negm(u), scale=1.0))
            sb.rd(V, "sq_sc")
            sb.wr(V, ("st_c", u))
            V.mark(nc.vector.reduce_sum(stdv(u), sq_sc[:, :], axis=AX.X))
        for u in range(n):
            sb.rd(SC, ("st_c", u), "ones")
            sb.wr(SC, ("st_d", u))
            SC.mark(nc.scalar.activation(stdv(u), stdv(u), AF.Ln,
                                         bias=epsc[:, :], scale=1.0 / D))
        for u in range(n):
            sb.rd(SC, ("st_d", u))
            sb.wr(SC, ("st_e", u))
            SC.mark(nc.scalar.activation(rstd(u), stdv(u), AF.Exp,
                                         scale=-0.5))
        for u in range(n):
            sb.rd(V, skey, ("st_b", u), ("st_e", u))
            V.mark(nc.vector.tensor_scalar(out_aps[u], x_aps[u], negm(u),
                                           rstd(u), ALU.add, ALU.mult))

    with ExitStack() as p0:
        wrows = p0.enter_context(nc.sbuf_tensor("p0_rows", [128, 24, D], BF16))
        colbuf = p0.enter_context(nc.sbuf_tensor("p0_col", [128, 3 * D], BF16))
        ps_t = p0.enter_context(nc.psum_tensor("p0_pst", [128, 512], BF16))

        for wname, w_h, cache, nrt, scale in (
            ("wqkv", wqkv_h, wqkvT_c, 24, None),
        ):
            src = w_h.ap().rearrange("(r p) d -> p r d", p=128)
            sb.wr(GP, "p0rows")
            GP.mark(nc.gpsimd.dma_start(wrows[:, 0:nrt, :], src))
            for c in range(8):
                for r0 in range(0, nrt, 4):
                    rblk = min(4, nrt - r0)
                    sb.rd(T, "p0rows", "consts")
                    sb.wr(T, "p0ps")
                    ins = None
                    for r in range(r0, r0 + rblk):
                        ins = nc.tensor.transpose(
                            ps_t[:, (r - r0) * 128:(r - r0 + 1) * 128],
                            wrows[:, r, c * 128:(c + 1) * 128], ident[:, :])
                    T.mark(ins)
                    sb.rd(V, "p0ps")
                    sb.wr(V, ("p0cb", r0))
                    V.mark(nc.vector.tensor_copy(
                        colbuf[:, r0 * 128:(r0 + rblk) * 128],
                        ps_t[:, 0:rblk * 128]))
                sb.rd(SY, *[("p0cb", r0) for r0 in range(0, nrt, 4)])
                sb.wr(SY, (wname + "T_c", c))
                SY.mark(nc.sync.dma_start(cache.ap()[:, c, 0:nrt * 128],
                                          colbuf[:, 0:nrt * 128]))
        sb.barrier()

    with ExitStack() as pab:
        a_tm = pab.enter_context(nc.sbuf_tensor("a_tm", [128, NU, NH, 65],
                                                BF16))
        sb.wr(V, "aones")
        V.mark(nc.vector.memset(
            a_tm[:, :, :, 64:65].rearrange("p a b c -> p (a b c)"), 1.0))

        with ExitStack() as pa:
            sbA = lambda n, s, d: pa.enter_context(nc.sbuf_tensor(n, s, d))
            psA = lambda n, s, d=F32: pa.enter_context(nc.psum_tensor(n, s, d))
            wqkvT = sbA("wqkvT", [128, 8, 3 * D], BF16)
            x_tm = sbA("x_tm", [128, TU, D], F32)
            q_tm = sbA("q_tm", [128, TU, D], BF16)
            xn_tm = q_tm
            xn_fm = sbA("xn_fm", [128, 8, 512], BF16)
            k_tm = sbA("k_tm", [128, TU, D], BF16)
            v_tm = sbA("v_tm", [128, TU, D], BF16)
            q_fm = sbA("q_fm", [128, 8, 512], BF16)
            k_fm = sbA("k_fm", [128, 8, 512], BF16)
            k_last = sbA("k_last", [128, 8, 15], BF16)
            v_last = sbA("v_last", [128, D], BF16)
            pbuf = sbA("pbuf", [128, NH, 256], BF16)
            pt_sb = [sbA("pt_sb%d" % i, [128, 256], BF16) for i in range(2)]
            wsum = sbA("wsum", [128, NH], F32)
            wrec = sbA("wrec", [128, NH], F32)
            rsc = sbA("rsc", [128, 2, 256], BF16)
            band = sbA("band", [128, 143], F32)
            band0 = sbA("band0", [128, 143], F32)
            ct8 = sbA("ct8", [128, NU, 8], F32)
            st8 = sbA("st8", [128, NU, 8], F32)
            bqkv_row = sbA("bqkv_row", [1, 3 * D], BF16)
            wsm = [sbA("wsm%d" % i, [128, 4, 128], BF16) for i in range(2)]
            rtmp = x_tm[:, 0, 0:256]
            posug = x_tm[:, 1, 0:NU]
            i8s = x_tm[:, 1, 64:72]
            rlad = x_tm[:, 1, 128:141]
            cwb = [sbA("cwb%d" % i, [128, 512], BF16) for i in range(2)]
            ps_mm = psA("ps_mm", [128, 1024])
            ps_tp = psA("ps_tp", [128, 2048], BF16)
            ps_w = psA("ps_w", [128, 1024])
            ps_a = psA("ps_a", [128, 1024])

            sb.wr(SY, "constsA")
            _n0 = SY.n
            SY.mark(nc.sync.dma_start(band[:, :], band_h[:, :]))
            SY.mark(nc.sync.dma_start(band0[:, :], band0_h[:, :]))
            SY.mark(nc.sync.dma_start(posug[:, :], posug_h[:, :]))
            SY.mark(nc.sync.dma_start(i8s[:, :], i8_h[:, :]))
            sb.bind_range(SY, _n0, "constsA")
            sb.wr(SY, "sq_sc")
            SY.mark(nc.sync.dma_start(
                sq_sc[0:1, 0:D], bqkv_h.ap()[0:D].rearrange("(a f) -> a f",
                                                            a=1)))
            sb.wr(GP, "constsA_gp")
            GP.mark(nc.gpsimd.dma_start(
                bqkv_row[0:1, D:3 * D],
                bqkv_h.ap()[D:3 * D].rearrange("(a f) -> a f", a=1)))
            sb.rd(SC, "sq_sc")
            sb.wr(SC, "constsA2")
            SC.mark(nc.scalar.activation(bqkv_row[0:1, 0:D], sq_sc[0:1, 0:D],
                                         AF.Copy, scale=1.0))
            sb.rd(SY, *[("wqkvT_c", c) for c in range(8)])
            sb.wr(SY, "wqkvT")
            SY.mark(nc.sync.dma_start(wqkvT[:, :, :], wqkvT_c.ap()[:, :, :]))
            sb.wr(V, "klast", "vlast")
            nc.vector.memset(k_last[:, :, :].rearrange("p a b -> p (a b)"), 0.0)
            nc.vector.memset(v_last[:, :], 0.0)
            sb.wr(V, "pbufz")
            V.mark(nc.vector.memset(
                pbuf[:, :, :].rearrange("p a b -> p (a b)"), 0.0))

            ct8f = ct8[:, :, :].rearrange("p a b -> p (a b)")
            st8f = st8[:, :, :].rearrange("p a b -> p (a b)")
            sb.rd(V, "constsA")
            sb.wr(V, "rop0")
            V.mark(nc.vector.memset(rlad[:, 0:1], 1.0))
            sb.rd(SC, "rop0")
            sb.wr(SC, "rop1")
            SC.mark(nc.scalar.activation(rlad[:, 0:1], rlad[:, 0:1],
                                         AF.Arctan))
            sb.rd(V, "rop1")
            sb.wr(V, "rop2")
            V.mark(nc.vector.tensor_scalar_mul(rlad[:, 1:2], rlad[:, 0:1],
                                               4.0))
            sb.rd(V, "rop2")
            sb.wr(V, "rop2")
            V.mark(nc.vector.tensor_scalar_mul(rlad[:, 0:1], rlad[:, 0:1],
                                               2.0))
            for j in range(10):
                sb.rd(V, "rop2")
                sb.wr(V, "rop2")
                V.mark(nc.vector.tensor_scalar_mul(
                    rlad[:, 2 + j:3 + j], rlad[:, 1 + j:2 + j], 2.0))
            sb.wr(V, "rop3")
            V.mark(nc.vector.memset(rlad[:, 12:13], 10000.0))
            sb.rd(SC, "rop3")
            sb.wr(SC, "rop4")
            SC.mark(nc.scalar.activation(rlad[:, 12:13], rlad[:, 12:13],
                                         AF.Ln))
            sb.rd(V, "rop4", "constsA")
            sb.wr(V, "rop5")
            V.mark(nc.vector.tensor_scalar(i8s[:, :], i8s[:, :],
                                           rlad[:, 12:13], None, ALU.mult))
            sb.rd(SC, "rop5")
            sb.wr(SC, "rop6")
            SC.mark(nc.scalar.activation(i8s[:, :], i8s[:, :], AF.Exp,
                                         scale=-1.0))
            for u in range(NU):
                sb.rd(V, "rop6", "constsA")
                sb.wr(V, ("ang", u))
                V.mark(nc.vector.tensor_scalar(
                    ct8[:, u, :], i8s[:, :], posug[:, u:u + 1], None,
                    ALU.mult))
            sb.w["ang2"] = (V, V.n, V.n)
            for j in range(9, -1, -1):
                cj = rlad[:, 2 + j:3 + j]
                sb.rd(V, "ang2")
                sb.wr(V, "rsel")
                V.mark(nc.vector.tensor_scalar(rtmp[:, :], ct8f, cj, cj,
                                               ALU.is_ge, ALU.mult))
                sb.rd(V, "rsel")
                sb.wr(V, "ang2")
                V.mark(nc.vector.tensor_sub(ct8f, ct8f, rtmp[:, :]))
            sb.rd(V, "ang2")
            sb.wr(V, "rsel")
            V.mark(nc.vector.tensor_scalar(rtmp[:, :], ct8f, rlad[:, 1:2],
                                           rlad[:, 2:3], ALU.is_ge,
                                           ALU.mult))
            sb.rd(V, "rsel")
            sb.wr(V, "sarg")
            V.mark(nc.vector.tensor_sub(st8f, ct8f, rtmp[:, :]))
            sb.rd(V, "ang2")
            sb.wr(V, "carg")
            V.mark(nc.vector.tensor_scalar(rtmp[:, :], ct8f, rlad[:, 0:1],
                                           None, ALU.add))
            for pcol in (2, 1):
                sb.rd(V, "carg")
                sb.wr(V, "csel")
                V.mark(nc.vector.tensor_scalar(ct8f, rtmp[:, :],
                                               rlad[:, pcol:pcol + 1],
                                               rlad[:, 2:3], ALU.is_ge,
                                               ALU.mult))
                sb.rd(V, "csel")
                sb.wr(V, "carg")
                V.mark(nc.vector.tensor_sub(rtmp[:, :], rtmp[:, :], ct8f))
            sb.rd(SC, "sarg")
            sb.wr(SC, "ropetab")
            SC.mark(nc.scalar.activation(st8f, st8f, AF.Sin))
            sb.rd(SC, "carg")
            sb.wr(SC, "ropetab")
            SC.mark(nc.scalar.activation(ct8f, rtmp[:, :], AF.Sin))

            cw_units = [(wname, w_h, cache, c, r0)
                        for wname, w_h, cache in (
                            ("wka", wka_h, wkaT_c),
                            ("wqa", wqa_h, wqaT_c),
                            ("wd", wd_h, wdT_c))
                        for c in range(8) for r0 in (0, 4)]
            cw_ctr = [0]

            def emit_cache_unit():
                wname, w_h, cache, c, r0 = cw_units.pop(0)
                ci = cw_ctr[0] % 2
                cw_ctr[0] += 1
                sb.wr(GP, ("cwsm", ci))
                GP.mark(nc.gpsimd.dma_start(
                    wsm[ci][:, :, :],
                    w_h.ap().rearrange("(r p) d -> p r d", p=128)[
                        :, r0:r0 + 4, c * 128:(c + 1) * 128]))
                slot = ci
                tb = slot * 1024
                sb.rd(T, ("cwsm", ci), "consts")
                sb.wr(T, ("ps_tp", slot))
                ins = None
                for r in range(4):
                    ins = nc.tensor.transpose(
                        ps_tp[:, tb + r * 128:tb + (r + 1) * 128],
                        wsm[ci][:, r, :], ident[:, :])
                T.mark(ins)
                ceng = V if ci == 0 else SC
                sb.rd(ceng, ("ps_tp", slot))
                sb.wr(ceng, ("cwb", ci))
                if ceng is V:
                    ceng.mark(nc.vector.tensor_copy(
                        cwb[ci][:, :], ps_tp[:, tb:tb + 512]))
                else:
                    ceng.mark(nc.scalar.copy(
                        cwb[ci][:, :], ps_tp[:, tb:tb + 512]))
                sb.rd(SY, ("cwb", ci))
                sb.wr(SY, (wname + "T_c", c, r0))
                SY.mark(nc.sync.dma_start(
                    cache.ap()[:, c, r0 * 128:(r0 + 4) * 128],
                    cwb[ci][:, :]))

            for g in range(NT):
                sb.rd(SY, "ropetab")
                sb.wr(SY, "x_tm")
                SY.mark(nc.sync.dma_start(
                    x_tm[:, :, :],
                    xf_h.ap()[g * 512:(g + 1) * 512, :].rearrange(
                        "(u p) d -> p u d", p=128)))
                layer_norm_batch([x_tm[:, u, :] for u in range(TU)],
                                 [xn_tm[:, u, :] for u in range(TU)], "x_tm")
                for u in range(TU):
                    mk = V.n - (TU - 1 - u)
                    sb.w[("q", u, 0)] = (V, mk, mk)
                    sb.w[("q", u, 1)] = (V, mk, mk)
                for c in range(8):
                    slot = c % 2
                    tb = slot * 1024
                    sb.rd(T, "consts", *[("q", u, 0 if c < 4 else 1)
                                         for u in range(TU)])
                    sb.wr(T, ("ps_tp", slot))
                    ins = None
                    for u in range(TU):
                        ins = nc.tensor.transpose(
                            ps_tp[:, tb + u * 128:tb + (u + 1) * 128],
                            xn_tm[:, u, c * 128:(c + 1) * 128], ident[:, :])
                    T.mark(ins)
                    ceng = V if c % 2 == 0 else SC
                    sb.rd(ceng, ("ps_tp", slot))
                    sb.wr(ceng, ("xn_fm", c))
                    if ceng is V:
                        ceng.mark(nc.vector.tensor_copy(
                            xn_fm[:, c, :], ps_tp[:, tb:tb + 512]))
                    else:
                        ceng.mark(nc.scalar.copy(
                            xn_fm[:, c, :], ps_tp[:, tb:tb + 512]))
                for u in range(TU):
                    for oc in range(6):
                        slot = (u * 6 + oc) % 2
                        sb.rd(T, "wqkvT", "constsA2", "constsA_gp", "ones",
                              *[("xn_fm", c) for c in range(8)])
                        sb.wr(T, ("ps_mm", slot))
                        for c in range(8):
                            nc.tensor.matmul(
                                ps_mm[:, slot * 512:(slot + 1) * 512],
                                xn_fm[:, c, u * 128:(u + 1) * 128],
                                wqkvT[:, c, oc * 512:(oc + 1) * 512],
                                start=(c == 0), stop=False)
                        T.mark(nc.tensor.matmul(
                            ps_mm[:, slot * 512:(slot + 1) * 512],
                            ones_bf[0:1, :],
                            bqkv_row[0:1, oc * 512:(oc + 1) * 512],
                            start=False, stop=True))
                        dst = (q_tm if oc < 2 else k_tm if oc < 4 else v_tm)
                        dcol = (oc % 2) * 512
                        okey = ("q", u, oc) if oc < 4 else ("v", u, oc)
                        sb.rd(SC, ("ps_mm", slot))
                        sb.wr(SC, okey)
                        SC.mark(nc.scalar.copy(
                            dst[:, u, dcol:dcol + 512],
                            ps_mm[:, slot * 512:(slot + 1) * 512]))
                ug0 = g * TU
                for u in range(TU):
                    for t_tm, oc0 in ((q_tm, 0), (k_tm, 2)):
                        sb.rd(V, ("q", u, oc0), ("q", u, oc0 + 1),
                              "constsA", "ropetab")
                        sb.wr(V, "rsc")
                        hd = t_tm[:, u, :].rearrange("p (h d) -> p h d", d=HD)
                        lo, hi = hd[:, :, 0:8], hd[:, :, 8:16]
                        ctu = ct8[:, ug0 + u:ug0 + u + 1, :]
                        stu = st8[:, ug0 + u:ug0 + u + 1, :]
                        r2 = rsc[:, 0, :].rearrange("p (h d) -> p h d", d=16)
                        r3 = rsc[:, 1, :].rearrange("p (h d) -> p h d", d=16)
                        rlo, tlo = r2[:, :, 0:8], r2[:, :, 8:16]
                        rhi, thi = r3[:, :, 0:8], r3[:, :, 8:16]
                        cb, _ = bass.broadcast_tensor_aps(ctu, lo)
                        sbr, _ = bass.broadcast_tensor_aps(stu, lo)
                        nc.vector.tensor_mul(rlo, lo, cb)
                        nc.vector.tensor_mul(tlo, hi, sbr)
                        nc.vector.tensor_mul(rhi, hi, cb)
                        V.mark(nc.vector.tensor_mul(thi, lo, sbr))
                        sb.rd(V, "rsc")
                        sb.wr(V, ("q", u, oc0), ("q", u, oc0 + 1))
                        nc.vector.tensor_sub(lo, rlo, tlo)
                        V.mark(nc.vector.tensor_add(hi, rhi, thi))
                for t_tm, t_fm, tag, oc0 in ((q_tm, q_fm, "qfm", 0),
                                             (k_tm, k_fm, "kfm", 2)):
                    for c in range(8):
                        slot = c % 2
                        tb = slot * 1024
                        sb.rd(T, "consts", *[("q", u, oc0 + (1 if c >= 4 else 0))
                                             for u in range(TU)])
                        sb.wr(T, ("ps_tp", slot))
                        ins = None
                        for u in range(TU):
                            ins = nc.tensor.transpose(
                                ps_tp[:, tb + u * 128:tb + (u + 1) * 128],
                                t_tm[:, u, c * 128:(c + 1) * 128], ident[:, :])
                        T.mark(ins)
                        ceng = V if c % 2 == 0 else SC
                        sb.rd(ceng, ("ps_tp", slot))
                        sb.wr(ceng, (tag, c))
                        if ceng is V:
                            ceng.mark(nc.vector.tensor_copy(
                                t_fm[:, c, :], ps_tp[:, tb:tb + 512]))
                        else:
                            ceng.mark(nc.scalar.copy(
                                t_fm[:, c, :], ps_tp[:, tb:tb + 512]))
                for u in range(TU):
                    for h in range(NH):
                        ro = (h % 2) * 64
                        hc = h // 2
                        slot = h % 2
                        lhs = q_fm[ro:ro + 64, hc, u * 128:(u + 1) * 128]
                        sb.rd(T, ("qfm", hc), ("kfm", hc))
                        if u == 0:
                            sb.rd(T, "klast")
                        sb.wr(T, ("ps_w", slot))
                        if u == 0:
                            nc.tensor.matmul(
                                ps_w[:, slot * 512:slot * 512 + 15], lhs,
                                k_last[ro:ro + 64, hc, :],
                                start=True, stop=True)
                            T.mark(nc.tensor.matmul(
                                ps_w[:, slot * 512 + 15:slot * 512 + 143],
                                lhs, k_fm[ro:ro + 64, hc, 0:128],
                                start=True, stop=True))
                        else:
                            T.mark(nc.tensor.matmul(
                                ps_w[:, slot * 512:slot * 512 + 143], lhs,
                                k_fm[ro:ro + 64, hc, u * 128 - 15:u * 128 + 128],
                                start=True, stop=True))
                        bnd = band0 if (g == 0 and u == 0) else band
                        sb.rd(V, ("ps_w", slot), "constsA")
                        sb.wr(V, ("ps_w", slot))
                        V.mark(nc.vector.tensor_add(
                            ps_w[:, slot * 512:slot * 512 + 143],
                            ps_w[:, slot * 512:slot * 512 + 143], bnd[:, :]))
                        sb.rd(SC, ("ps_w", slot), "pbufz")
                        sb.wr(SC, ("pbuf", h), ("wsum", h))
                        SC.mark(nc.scalar.activation(
                            pbuf[:, h, 113:256],
                            ps_w[:, slot * 512:slot * 512 + 143],
                            AF.Exp, scale=0.125,
                            accum_out=wsum[:, h:h + 1]))
                    sb.rd(V, *[("wsum", h) for h in range(NH)])
                    sb.wr(V, "wrec")
                    V.mark(nc.vector.reciprocal(wrec[:, :], wsum[:, :]))
                    V.selfbar()
                    def w_pt(h):
                        slot = h % 2
                        tb = slot * 1024
                        sb.rd(T, ("pbuf", h), "consts")
                        sb.wr(T, ("ps_tp", slot))
                        nc.tensor.transpose(ps_tp[:, tb:tb + 128],
                                            pbuf[:, h, 0:128], ident[:, :])
                        T.mark(nc.tensor.transpose(
                            ps_tp[:, tb + 128:tb + 256],
                            pbuf[:, h, 128:256], ident[:, :]))
                        ceng = V if h % 2 == 0 else SC
                        sb.rd(ceng, ("ps_tp", slot))
                        sb.wr(ceng, ("pt_sb", slot))
                        if ceng is V:
                            ceng.mark(nc.vector.tensor_copy(
                                pt_sb[slot][:, 0:256], ps_tp[:, tb:tb + 256]))
                        else:
                            ceng.mark(nc.scalar.copy(
                                pt_sb[slot][:, 0:256], ps_tp[:, tb:tb + 256]))

                    def w_pv(h):
                        slot = h % 2
                        vkey = ("v", u, 4 + (1 if h >= 8 else 0))
                        if u > 0:
                            pv_rhs1 = v_tm[:, u - 1, h * 64:h * 64 + 64]
                            pkey = ("v", u - 1, 4 + (1 if h >= 8 else 0))
                        else:
                            pv_rhs1 = v_last[:, h * 64:h * 64 + 64]
                            pkey = "vlast"
                        sb.rd(T, ("pt_sb", slot), vkey, pkey)
                        sb.wr(T, ("ps_a", slot))
                        nc.tensor.matmul(ps_a[:, slot * 512:slot * 512 + 64],
                                         pt_sb[slot][:, 0:128], pv_rhs1,
                                         start=True, stop=False)
                        T.mark(nc.tensor.matmul(
                            ps_a[:, slot * 512:slot * 512 + 64],
                            pt_sb[slot][:, 128:256],
                            v_tm[:, u, h * 64:h * 64 + 64],
                            start=False, stop=True))
                        sb.rd(V, ("ps_a", slot), "wrec")
                        sb.wr(V, ("a_tm", ug0 + u))
                        V.mark(nc.vector.tensor_scalar(
                            a_tm[:, ug0 + u, h, 0:64],
                            ps_a[:, slot * 512:slot * 512 + 64],
                            wrec[:, h:h + 1], None, ALU.mult))

                    for h in range(NH):
                        w_pt(h)
                        if h > 0:
                            w_pv(h - 1)
                    w_pv(NH - 1)
                sb.rd(V, ("kfm", 0))
                for c in range(8):
                    sb.rd(V, ("kfm", c))
                sb.wr(V, "klast")
                V.mark(nc.vector.tensor_copy(k_last[:, :, :],
                                              k_fm[:, :, 497:512]))
                sb.rd(V, ("v", TU - 1, 4), ("v", TU - 1, 5))
                sb.wr(V, "vlast")
                V.mark(nc.vector.tensor_copy(v_last[:, :], v_tm[:, TU - 1, :]))
                for _ in range(6):
                    if cw_units:
                        emit_cache_unit()

            if dump == "A":
                sb.rd(SY, *[("a_tm", uu) for uu in range(NU)])
                SY.mark(nc.sync.dma_start(atm_d.ap()[:, :, :, :],
                                          a_tm[:, :, :, :]))
                sb.barrier()
                SY.mark(nc.sync.dma_start(q_d.ap()[:, :, :], q_tm[:, :, :]))
                SY.mark(nc.sync.dma_start(k_d.ap()[:, :, :], k_tm[:, :, :]))
                SY.mark(nc.sync.dma_start(v_d.ap()[:, :, :], v_tm[:, :, :]))
                SY.mark(nc.sync.dma_start(pb_d.ap()[:, :, :], pbuf[:, :, :]))
                SY.mark(nc.sync.dma_start(ws_d.ap()[:, :], wsum[:, :]))
                SY.mark(nc.sync.dma_start(xnf_d.ap()[:, :, :], xn_fm[:, :, :]))
                SY.mark(nc.sync.dma_start(wq_d.ap()[:, :, :], wqkvT[:, :, :]))
                SY.mark(nc.sync.dma_start(st_d.ap()[:, :], stats[:, :]))
                SY.mark(nc.sync.dma_start(sq_d.ap()[:, :], sq_sc[:, :]))
                SY.drain()
                GP.drain()
                return nc
            sb.barrier()

        with ExitStack() as pb:
            sbB = lambda n, s, d: pb.enter_context(nc.sbuf_tensor(n, s, d))
            psB = lambda n, s, d=F32: pb.enter_context(nc.psum_tensor(n, s, d))
            ka_fm = sbB("ka_fm", [128, 8, S], BF16)
            warena = [sbB("warena%d" % i, [128, 8, D], BF16) for i in range(2)]
            a_fm = sbB("a_fm", [128, 8, 512], BF16)
            qa_fm = sbB("qa_fm", [128, 8, 512], BF16)
            p_sb = sbB("p_sb", [128, 2048], BF16)
            out_tm = sbB("out_tm", [128, TU, D], BF16)
            out_fm = sbB("out_fm", [128, 8, 128], BF16)
            x_u = sbB("x_u", [128, D], F32)
            xn2_tm = sbB("xn2_tm", [128, D], BF16)
            xn2_fmu = sbB("xn2_fmu", [128, 8, 128], BF16)
            s_run = sbB("s_run", [128, TU * NH], F32)
            orc = sbB("orc", [128, NH], F32)
            diagm = sbB("diagm", [128, 128], BF16)
            bqa_col = sbB("bqa_col", [128, 8], F32)
            bka_col = sbB("bka_col", [128, 8], F32)
            bd_row = sbB("bd_row", [1, D], BF16)
            ps_sc = psB("ps_sc", [128, 2048])
            ps_pv = psB("ps_pv", [128, 1024])
            ps_tpB = psB("ps_tpB", [128, 2048], BF16)
            ps_y = ps_sc

            sb.wr(SY, "constsB")
            SY.mark(nc.sync.dma_start(diagm[:, :], diag_h[:, :]))
            sb.wr(GP, "constsB2")
            _n0 = GP.n
            GP.mark(nc.gpsimd.dma_start(
                bd_row[:, :], bd_h.ap().rearrange("(a d) -> a d", a=1)))
            with nc.allow_non_contiguous_dma(reason="tiny col packs, once"):
                GP.mark(nc.gpsimd.dma_start(
                    bqa_col[:, :], bqa_h.ap().rearrange("(c p) -> p c", p=128)))
                GP.mark(nc.gpsimd.dma_start(
                    bka_col[:, :], bka_h.ap().rearrange("(c p) -> p c", p=128)))
            sb.bind_range(GP, _n0, "constsB2")
            sb.rd(V, "constsB2")
            sb.wr(V, "constsB3")
            V.mark(nc.vector.tensor_scalar_mul(bqa_col[:, :], bqa_col[:, :],
                                               0.125))
            sb.rd(SY, *[("wkaT_c", c, r0)
                        for c in range(8) for r0 in (0, 4)])
            sb.wr(SY, ("warena", 0))
            SY.mark(nc.sync.dma_start(warena[0][:, :, :],
                                      wkaT_c.ap()[:, :, :]))

            def emit_phase_b(j):
                mine = MINE[j]
                ka_ai, qa_ai, wd_ai = 0, 1, 1

                def load_arena(cache, key):
                    sb.rd(SY, *[(key + "T_c", c, r0)
                                for c in range(8) for r0 in (0, 4)])
                    sb.wr(SY, ("warena", 1))
                    SY.mark(nc.sync.dma_start(warena[1][:, :, :],
                                              cache.ap()[:, :, :]))

                for g in range(NT):
                    if g in mine:
                        load_arena(wqaT_c, "wqa")
                    for hc in range(8):
                        sb.rd(T, "consts", *[("a_tm", g * TU + u)
                                             for u in range(TU)])
                        sb.wr(T, ("ps_tpB", 0))
                        ins = None
                        for u in range(TU):
                            for hh in range(2):
                                ins = nc.tensor.transpose(
                                    ps_tpB[hh * 64:hh * 64 + 64,
                                           u * 128:(u + 1) * 128],
                                    a_tm[:, g * TU + u, hc * 2 + hh, 0:64],
                                    ident[:, :])
                        T.mark(ins)
                        sb.rd(V, ("ps_tpB", 0))
                        sb.wr(V, ("a_fm", hc))
                        V.mark(nc.vector.tensor_copy(a_fm[:, hc, :],
                                                     ps_tpB[:, 0:512]))
                    projs = [("ka", ka_ai, bka_col)]
                    if g in mine:
                        projs.append(("qa", qa_ai, bqa_col))
                    for pname, ai, bcol in projs:
                        for oc in range(8):
                            slot = oc % 2
                            sb.rd(T, ("warena", ai),
                                  *[("a_fm", c) for c in range(8)])
                            sb.wr(T, ("ps_sc", slot))
                            ins = None
                            for c in range(8):
                                ins = nc.tensor.matmul(
                                    ps_y[:, slot * 512:(slot + 1) * 512],
                                    warena[ai][:, c, oc * 128:(oc + 1) * 128],
                                    a_fm[:, c, :], start=(c == 0),
                                    stop=(c == 7))
                            T.mark(ins)
                            sb.rd(V, ("ps_sc", slot), "constsB2", "constsB3")
                            if pname == "ka":
                                dst = ka_fm[:, oc, g * 512:(g + 1) * 512]
                                sb.wr(V, ("ka", g, oc))
                            else:
                                dst = qa_fm[:, oc, :]
                                sb.wr(V, ("qa", oc))
                            if pname == "qa":
                                V.mark(nc.vector.tensor_scalar(
                                    dst,
                                    ps_y[:, slot * 512:(slot + 1) * 512],
                                    0.125, bcol[:, oc:oc + 1],
                                    ALU.mult, ALU.add))
                            else:
                                V.mark(nc.vector.tensor_scalar(
                                    dst,
                                    ps_y[:, slot * 512:(slot + 1) * 512],
                                    bcol[:, oc:oc + 1], None, ALU.add))
                    if g not in mine:
                        continue
                    lt = mine.index(g)
                    K4 = 4 * g + 4
                    chunks = [(h, kc) for h in range(NH)
                              for kc in range(K4)]
                    cslot = {}
                    sctr = [0]

                    def emit_qk(ch):
                        h, kc = ch
                        ro = (h % 2) * 64
                        hc = h // 2
                        slot = sctr[0] % 4
                        sctr[0] += 1
                        cslot[ch] = slot
                        q0 = max(0, kc - 4 * g) * 128
                        sb.rd(T, ("qa", hc), ("ka", kc // 4, hc))
                        sb.wr(T, ("ps_sc", slot))
                        T.mark(nc.tensor.matmul(
                            ps_sc[:, slot * 512 + q0:(slot + 1) * 512],
                            ka_fm[ro:ro + 64, hc, kc * 128:(kc + 1) * 128],
                            qa_fm[ro:ro + 64, hc, q0:512],
                            start=True, stop=True))
                        if kc >= 4 * g:
                            sb.rd(V, ("ps_sc", slot), "constsB")
                            sb.wr(V, ("ps_sc", slot))
                            V.mark(nc.vector.tensor_add(
                                ps_sc[:, slot * 512 + q0:
                                      slot * 512 + q0 + 128],
                                ps_sc[:, slot * 512 + q0:
                                      slot * 512 + q0 + 128],
                                diagm[:, :]))

                    def emit_exp(ch):
                        h, kc = ch
                        slot = cslot[ch]
                        q0 = max(0, kc - 4 * g) * 128
                        if q0 > 0:
                            sb.wr(V, ("p_sbz", slot))
                            V.mark(nc.vector.memset(
                                p_sb[:, slot * 512:slot * 512 + q0], 0.0))
                        sb.rd(SC, ("ps_sc", slot))
                        sb.wr(SC, ("p_sb", slot))
                        SC.mark(nc.scalar.activation(
                            p_sb[:, slot * 512 + q0:(slot + 1) * 512],
                            ps_sc[:, slot * 512 + q0:(slot + 1) * 512],
                            AF.Exp))

                    def emit_pv(ch):
                        h, kc = ch
                        slot = cslot[ch]
                        pvs = (h % 2) * 512
                        sb.rd(T, ("p_sb", slot), ("p_sbz", slot),
                              ("a_tm", kc), "aones")
                        sb.wr(T, ("ps_pv", h % 2))
                        ins = None
                        for uu in range(TU):
                            ins = nc.tensor.matmul(
                                ps_pv[:, pvs + uu * 128:
                                      pvs + uu * 128 + 65],
                                p_sb[:, slot * 512 + uu * 128:
                                     slot * 512 + (uu + 1) * 128],
                                a_tm[:, kc, h, 0:65],
                                start=(kc == 0 and uu == 0),
                                stop=(kc == K4 - 1 and uu == TU - 1))
                        T.mark(ins)

                    def emit_out(h):
                        for uu in range(TU):
                            pvs = (h % 2) * 512 + uu * 128
                            sb.rd(V, ("ps_pv", h % 2))
                            sb.wr(V, ("s_run", uu, h))
                            V.mark(nc.vector.tensor_copy(
                                s_run[:, uu * NH + h:uu * NH + h + 1],
                                ps_pv[:, pvs + 64:pvs + 65]))
                            sb.rd(V, ("ps_pv", h % 2))
                            sb.wr(V, ("out_tm", uu, h))
                            V.mark(nc.vector.tensor_copy(
                                out_tm[:, uu, h * 64:h * 64 + 64],
                                ps_pv[:, pvs:pvs + 64]))

                    emit_qk(chunks[0])
                    emit_qk(chunks[1])
                    if len(chunks) > 2:
                        emit_qk(chunks[2])
                    for i, ch in enumerate(chunks):
                        emit_exp(ch)
                        if i + 3 < len(chunks):
                            emit_qk(chunks[i + 3])
                        emit_pv(ch)
                        if ch[1] == K4 - 1:
                            emit_out(ch[0])
                    load_arena(wdT_c, "wd")
                    for u in range(TU):
                        sb.rd(V, *[("s_run", u, h) for h in range(NH)])
                        sb.wr(V, "orc")
                        V.mark(nc.vector.reciprocal(
                            orc[:, :], s_run[:, u * NH:(u + 1) * NH]))
                        for h in range(NH):
                            sb.rd(V, "orc", ("out_tm", u, h))
                            sb.wr(V, ("out_tm", u, h))
                            V.mark(nc.vector.tensor_scalar(
                                out_tm[:, u, h * 64:h * 64 + 64],
                                out_tm[:, u, h * 64:h * 64 + 64],
                                orc[:, h:h + 1], None, ALU.mult))
                        for c4 in range(2):
                            sb.rd(T, "consts",
                                  *[("out_tm", u, h) for h in range(NH)])
                            sb.wr(T, ("ps_tpB", 0))
                            ins = None
                            for i4 in range(4):
                                c = c4 * 4 + i4
                                ins = nc.tensor.transpose(
                                    ps_tpB[:, i4 * 128:(i4 + 1) * 128],
                                    out_tm[:, u, c * 128:(c + 1) * 128],
                                    ident[:, :])
                            T.mark(ins)
                            sb.rd(V, ("ps_tpB", 0))
                            sb.wr(V, ("out_fm", c4))
                            V.mark(nc.vector.tensor_copy(
                                out_fm[:, c4 * 4:(c4 + 1) * 4, :].rearrange(
                                    "p a b -> p (a b)"), ps_tpB[:, 0:512]))
                        for half in range(2):
                            sb.rd(T, ("warena", wd_ai), "constsB2", "ones",
                                  ("out_fm", 0), ("out_fm", 1))
                            sb.wr(T, ("ps_sc", half))
                            for c in range(8):
                                nc.tensor.matmul(
                                    ps_y[:, half * 512:(half + 1) * 512],
                                    out_fm[:, c, :],
                                    warena[wd_ai][:, c,
                                                  half * 512:(half + 1) * 512],
                                    start=(c == 0), stop=False)
                            T.mark(nc.tensor.matmul(
                                ps_y[:, half * 512:(half + 1) * 512],
                                ones_bf[0:1, :],
                                bd_row[0:1, half * 512:(half + 1) * 512],
                                start=False, stop=True))
                        sb.wr(SY, "x_u")
                        SY.mark(nc.sync.dma_start(
                            x_u[:, :], xs_h.ap()[lt, u * 128:(u + 1) * 128, :]))
                        sb.rd(V, "x_u", ("ps_sc", 0), ("ps_sc", 1))
                        sb.wr(V, "x_u")
                        V.mark(nc.vector.tensor_add(x_u[:, :], x_u[:, :],
                                                    ps_y[:, 0:1024]))
                        sb.rd(GP, "x_u")
                        sb.wr(GP, ("x2_sp", lt, u))
                        GP.mark(nc.gpsimd.dma_start(x2_sp.ap()[lt, :, u, :],
                                                    x_u[:, :]))
                        layer_norm_batch([x_u[:, :]], [xn2_tm[:, :]],
                                         "x_u")
                        sb.w["xn2_tm"] = (V, V.n, V.n)
                        for c4 in range(2):
                            sb.rd(T, "consts", "xn2_tm")
                            sb.wr(T, ("ps_tpB", 0))
                            ins = None
                            for i4 in range(4):
                                c = c4 * 4 + i4
                                ins = nc.tensor.transpose(
                                    ps_tpB[:, i4 * 128:(i4 + 1) * 128],
                                    xn2_tm[:, c * 128:(c + 1) * 128],
                                    ident[:, :])
                            T.mark(ins)
                            sb.rd(V, ("ps_tpB", 0))
                            sb.wr(V, ("xn2f", c4))
                            ins = None
                            for i4 in range(4):
                                c = c4 * 4 + i4
                                ins = nc.vector.tensor_copy(
                                    xn2_fmu[:, c, :],
                                    ps_tpB[:, i4 * 128:(i4 + 1) * 128])
                            V.mark(ins)
                        sb.rd(GP, ("xn2f", 0), ("xn2f", 1))
                        sb.wr(GP, ("xn2_sp", lt, u))
                        GP.mark(nc.gpsimd.dma_start(
                            xn2_sp.ap()[lt, :, :, u * 128:(u + 1) * 128],
                            xn2_fmu[:, :, :]))

            engs = [e.e for e in ALLE]
            pre = sb.state()
            posts = []
            for j in nc.Switch(engines=engs, index=[jreg] * len(engs), n=2):
                if j > 0:
                    posts.append(sb.state())
                    sb.restore(pre)
                emit_phase_b(j)
            posts.append(sb.state())
            assert posts[0][2] == posts[1][2], (
                f"switch bodies diverge: {posts[0][2]} vs {posts[1][2]}")
            for e in ALLE:
                (n0, w0, c0) = posts[0][3][e.name]
                (n1, w1, c1) = posts[1][3][e.name]
                assert n0 == n1, (e.name, n0, n1)
                if e.is_dma:
                    assert c0 == c1, (e.name, c0, c1)
                else:
                    e.last_sb = min(c0, c1)
                e.waited = {k: min(w0.get(k, 0), w1.get(k, 0))
                            for k in set(w0) | set(w1)}
            sb.barrier()

    if dump == "B":
        SY.drain()
        GP.drain()
        return nc

    with ExitStack() as pc:
        sbC = lambda n, s, d: pc.enter_context(nc.sbuf_tensor(n, s, d))
        psC = lambda n, s, d=F32: pc.enter_context(nc.psum_tensor(n, s, d))
        mixed = sbC("mixed", [128, 16, D], F32)
        xn2a = sbC("xn2a", [128, 8, 2048], BF16)
        ar1 = sbC("ar1", [128, 8, H2], BF16)
        ar2 = sbC("ar2", [128, 16, D], BF16)
        h1 = sbC("h1", [128, 16, 512], BF16)
        u8 = sbC("u8", [128, E], F32)
        csel = sbC("csel", [128, E], F32)
        rstat = sbC("rstat", [128, 8], F32)
        wbf = sbC("wbf", [128, E], BF16)
        w_all = sbC("w_all", [128, 16, E], F32)
        wT9 = sbC("wT9", [9, 2048], BF16)
        rwq_sb = sbC("rwq_sb", [128, 8, E], BF16)
        rb_row = sbC("rb_row", [1, E], BF16)
        sb1_col = sbC("sb1_col", [128, 16], F32)
        rb1_col = sbC("rb1_col", [128, E, 16], F32)
        rb2sb2 = sbC("rb2sb2", [9, D], BF16)
        xb2 = [sbC("xb2_%d" % i, [128, D], F32) for i in range(2)]
        outb = [sbC("outb_%d" % i, [128, D], F32) for i in range(2)]
        ps_h1 = psC("ps_h1", [128, 1024])
        ps_y2 = psC("ps_y2", [128, 1024])
        ps_r = psC("ps_r", [128, 512])
        ps_tw = psC("ps_tw", [128, 128], BF16)

        sb.wr(GP, "constsC")
        _n0 = GP.n
        GP.mark(nc.gpsimd.dma_start(rb2sb2[0:8, :], eb2_h[:, :]))
        GP.mark(nc.gpsimd.dma_start(
            rb2sb2[8:9, :], sb2_h.ap().rearrange("(a d) -> a d", a=1)))
        GP.mark(nc.gpsimd.dma_start(
            rb_row[:, :], rb_h.ap().rearrange("(a d) -> a d", a=1)))
        with nc.allow_non_contiguous_dma(reason="tiny col packs, once"):
            GP.mark(nc.gpsimd.dma_start(
                rwq_sb[:, :, :], rw_h.ap().rearrange("(c p) e -> p c e",
                                                     p=128)))
            GP.mark(nc.gpsimd.dma_start(
                sb1_col[:, :], sb1_h.ap().rearrange("(c p) -> p c", p=128)))
            for ee in range(E):
                GP.mark(nc.gpsimd.dma_start(
                    rb1_col[:, ee, :],
                    eb1_h.ap()[ee].rearrange("(c p) -> p c", p=128)))
        sb.bind_range(GP, _n0, "constsC")
        sb.wr(V, "wT9ones")
        V.mark(nc.vector.memset(wT9[:, :], 1.0))

        for lt in range(4):
            sb.rd(SY, ("xn2_sp", lt, 0), ("xn2_sp", lt, 1),
                  ("xn2_sp", lt, 2), ("xn2_sp", lt, 3))
            sb.wr(SY, ("xn2a", lt))
            SY.mark(nc.sync.dma_start(xn2a[:, :, lt * 512:(lt + 1) * 512],
                                      xn2_sp.ap()[lt, :, :, :]))

        for u16 in range(16):
            lt = u16 // 4
            sb.rd(T, ("xn2a", lt), "constsC", "ones")
            sb.wr(T, "ps_r")
            for c in range(8):
                nc.tensor.matmul(ps_r[:, 0:E],
                                 xn2a[:, c, u16 * 128:(u16 + 1) * 128],
                                 rwq_sb[:, c, :], start=(c == 0), stop=False)
            T.mark(nc.tensor.matmul(ps_r[:, 0:E], ones_bf[0:1, :],
                                    rb_row[0:1, :], start=False, stop=True))
            sb.rd(V, "ps_r")
            sb.wr(V, "rstat")
            V.mark(nc.vector.reduce_max(rstat[:, 0:1], ps_r[:, 0:E],
                                        axis=AX.X, negate=True))
            sb.rd(SC, "ps_r", "rstat")
            sb.wr(SC, "u8", "rsum")
            SC.mark(nc.scalar.activation(u8[:, :], ps_r[:, 0:E], AF.Exp,
                                         bias=rstat[:, 0:1],
                                         accum_out=rstat[:, 1:2]))
            sb.rd(V, "u8")
            sb.wr(V, "rm1")
            V.mark(nc.vector.reduce_max(rstat[:, 2:3], u8[:, :], axis=AX.X))
            V.selfbar()
            sb.rd(V, "u8", "rm1")
            sb.wr(V, "csel")
            V.mark(nc.vector.tensor_scalar(csel[:, :], u8[:, :],
                                           rstat[:, 2:3], None, ALU.is_ge))
            sb.rd(V, "csel")
            sb.wr(V, "u8")
            V.mark(nc.vector.tensor_sub(u8[:, :], u8[:, :], csel[:, :]))
            sb.rd(V, "u8")
            sb.wr(V, "rm2")
            V.mark(nc.vector.reduce_max(rstat[:, 3:4], u8[:, :], axis=AX.X))
            V.selfbar()
            sb.rd(V, "csel")
            sb.wr(V, "u8")
            V.mark(nc.vector.tensor_add(u8[:, :], u8[:, :], csel[:, :]))
            sb.rd(V, "u8", "rm2")
            sb.wr(V, "csel")
            V.mark(nc.vector.tensor_scalar(csel[:, :], u8[:, :],
                                           rstat[:, 3:4], None, ALU.is_ge))
            sb.rd(V, "rsum")
            sb.wr(V, "rrec")
            V.mark(nc.vector.reciprocal(rstat[:, 4:5], rstat[:, 1:2]))
            V.selfbar()
            sb.rd(V, "u8", "csel", "rrec")
            sb.wr(V, ("w_all", u16))
            nc.vector.tensor_mul(w_all[:, u16, :], u8[:, :], csel[:, :])
            V.mark(nc.vector.tensor_scalar(w_all[:, u16, :], w_all[:, u16, :],
                                           rstat[:, 4:5], None, ALU.mult))
            sb.rd(V, ("w_all", u16))
            sb.wr(V, "wbf")
            V.mark(nc.vector.tensor_copy(wbf[:, :], w_all[:, u16, :]))
            sb.rd(T, "wbf", "consts")
            sb.wr(T, "ps_tw")
            T.mark(nc.tensor.transpose(ps_tw[0:E, :], wbf[:, :], ident[:, :]))
            sb.rd(V, "ps_tw", "wT9ones")
            sb.wr(V, ("wT9", u16))
            V.mark(nc.vector.tensor_copy(wT9[0:E, u16 * 128:(u16 + 1) * 128],
                                         ps_tw[0:E, :]))
        V.selfbar()

        def load_weights(tag, e):
            if tag == "sh":
                w1src = sw1_h.ap().rearrange("(c p) h -> p c h", p=128)
                w2src = sw2_h.ap().rearrange("(c p) d -> p c d", p=128)
            else:
                w1src = ew1_h.ap()[e].rearrange("(c p) h -> p c h", p=128)
                w2src = ew2_h.ap()[e].rearrange("(c p) d -> p c d", p=128)
            sb.wr(GP, "ar1")
            GP.mark(nc.gpsimd.dma_start(ar1[:, :, :], w1src))
            sb.wr(GP, "ar2")
            GP.mark(nc.gpsimd.dma_start(ar2[:, :, :], w2src))

        def ffn_pass(tag, e):
            for tt in range(4):
                for oc in range(16):
                    slot = oc % 2
                    sb.rd(T, "ar1", ("xn2a", tt))
                    sb.wr(T, ("ps_h1", slot))
                    ins = None
                    for c in range(8):
                        ins = nc.tensor.matmul(
                            ps_h1[:, slot * 512:(slot + 1) * 512],
                            ar1[:, c, oc * 128:(oc + 1) * 128],
                            xn2a[:, c, tt * 512:(tt + 1) * 512],
                            start=(c == 0), stop=(c == 7))
                    T.mark(ins)
                    bcol = (sb1_col[:, oc:oc + 1] if tag == "sh"
                            else rb1_col[:, e, oc:oc + 1])
                    sb.rd(SC, ("ps_h1", slot), "constsC")
                    sb.wr(SC, ("h1", oc))
                    SC.mark(nc.scalar.activation(
                        h1[:, oc, :], ps_h1[:, slot * 512:(slot + 1) * 512],
                        AF.Gelu, bias=bcol))
                for ts in range(4):
                    u16 = tt * 4 + ts
                    for half in range(2):
                        slot = half
                        sb.rd(T, "ar2", *[("h1", oc) for oc in range(16)])
                        if tag == "sh":
                            sb.rd(T, ("wT9", u16), "constsC", "wT9ones")
                        sb.wr(T, ("ps_y2", slot))
                        ins = None
                        for c in range(16):
                            last = (c == 15) and tag != "sh"
                            ins = nc.tensor.matmul(
                                ps_y2[:, slot * 512:(slot + 1) * 512],
                                h1[:, c, ts * 128:(ts + 1) * 128],
                                ar2[:, c, half * 512:(half + 1) * 512],
                                start=(c == 0), stop=last)
                        if tag == "sh":
                            ins = nc.tensor.matmul(
                                ps_y2[:, slot * 512:(slot + 1) * 512],
                                wT9[:, u16 * 128:(u16 + 1) * 128],
                                rb2sb2[:, half * 512:(half + 1) * 512],
                                start=False, stop=True)
                        T.mark(ins)
                        sb.rd(V, ("ps_y2", slot))
                        sb.wr(V, ("mixed", u16, half))
                        dstm = mixed[:, u16, half * 512:(half + 1) * 512]
                        if tag == "sh":
                            V.mark(nc.vector.tensor_copy(
                                dstm, ps_y2[:, slot * 512:(slot + 1) * 512]))
                        else:
                            sb.rd(V, ("w_all", u16))
                            V.mark(nc.vector.scalar_tensor_tensor(
                                dstm, ps_y2[:, slot * 512:(slot + 1) * 512],
                                w_all[:, u16, e:e + 1], dstm,
                                ALU.mult, ALU.add))

        load_weights("sh", None)
        ffn_pass("sh", None)
        if dump == "C1":
            sb.barrier()
            SY.mark(nc.sync.dma_start(xn2a_d.ap()[:, :, :], xn2a[:, :, :]))
            SY.mark(nc.sync.dma_start(wall_d.ap()[:, :, :], w_all[:, :, :]))
            SY.mark(nc.sync.dma_start(mix_d.ap()[:, :, :], mixed[:, :, :]))
            SY.mark(nc.sync.dma_start(wt9_d.ap()[:, :], wT9[:, :]))
            SY.drain()
            GP.drain()
            return nc
        for e in range(E):
            load_weights(e, e)
            ffn_pass(e, e)

        for u16 in range(16):
            lt, u = u16 // 4, u16 % 4
            slot = u16 % 2
            sb.rd(SY, ("x2_sp", lt, u))
            sb.wr(SY, ("xb2", slot))
            SY.mark(nc.sync.dma_start(xb2[slot][:, :],
                                      x2_sp.ap()[lt, :, u, :]))
            sb.rd(V, ("xb2", slot), ("mixed", u16, 0), ("mixed", u16, 1))
            sb.wr(V, ("outb", slot))
            V.mark(nc.vector.tensor_add(outb[slot][:, :], xb2[slot][:, :],
                                        mixed[:, u16, :]))
            sb.rd(GP, ("outb", slot))
            sb.wr(GP, ("outd", u16))
            GP.mark(nc.gpsimd.dma_start(out_h.ap()[lt, :, u, :],
                                        outb[slot][:, :]))

    GP.drain()
    SY.drain()
    st_ctx.close()
    return nc


def _in_maps(inputs):
    x = np.ascontiguousarray(np.asarray(inputs["L0_attn_hidden_states"],
                                        np.float32))
    f32 = lambda k: np.ascontiguousarray(np.asarray(inputs[k], np.float32))
    maps = []
    for core in range(8):
        b, j = core // 2, core % 2
        maps.append({
            "xf": np.ascontiguousarray(x[b]),
            "xs": np.ascontiguousarray(
                np.stack([x[b, g * 512:(g + 1) * 512] for g in MINE[j]])),
            "jsel": np.full((1, 1), j, np.int32),
            "wqkv": f32("L0_attn_Wqkv"), "bqkv": f32("L0_attn_bqkv"),
            "wqa": f32("L0_attn_Wqa"), "bqa": f32("L0_attn_bqa"),
            "wka": f32("L0_attn_Wka"), "bka": f32("L0_attn_bka"),
            "wd": f32("L0_attn_Wd"), "bd": f32("L0_attn_bd"),
            "rw": f32("L0_ffn_router_w"), "rb": f32("L0_ffn_router_b"),
            "sw1": f32("L0_ffn_sw1"), "sb1": f32("L0_ffn_sb1"),
            "sw2": f32("L0_ffn_sw2"), "sb2": f32("L0_ffn_sb2"),
            "ew1": f32("L0_ffn_rw1"), "eb1": f32("L0_ffn_rb1"),
            "ew2": f32("L0_ffn_rw2"), "eb2": f32("L0_ffn_rb2"),
        })
    return maps


def kernel(**inputs):
    nc = build()
    maps = _in_maps(inputs)
    res = run_bass_kernel_spmd(nc, maps, core_ids=list(range(8)))
    out = np.zeros((4, S, D), np.asarray(res.results[0]["out"]).dtype)
    for core in range(8):
        b, j = core // 2, core % 2
        o = np.asarray(res.results[core]["out"])
        o = o.transpose(0, 2, 1, 3).reshape(4, 512, D)
        for lt, g in enumerate(MINE[j]):
            out[b, g * 512:(g + 1) * 512] = o[lt]
    return out


# version 17
# speedup vs baseline: 1.0268x; 1.0268x over previous
from contextlib import ExitStack

import numpy as np
import ml_dtypes

import concourse.bass as bass
import concourse.mybir as mybir
from concourse.bass_utils import run_bass_kernel_spmd

F32 = mybir.dt.float32
BF16 = mybir.dt.bfloat16
I32 = mybir.dt.int32
AF = mybir.ActivationFunctionType
ALU = mybir.AluOpType
AX = mybir.AxisListType

D = 1024
NH = 16
HD = 64
S = 4096
NT = 8
TU = 4
NU = 32
E = 8
H2 = 2048
NEG = -1.0e30
MINE = [[0, 3, 4, 7], [1, 2, 5, 6]]


class _Eng:

    def __init__(self, nc, eng, name):
        self.e = eng
        self.name = name
        self.sem = nc.alloc_semaphore("pg_" + name)
        self.n = 0
        self.waited = {}
        self.is_dma = False

    def mark(self, instr):
        instr.then_inc(self.sem, 1)
        self.n += 1
        return instr

    def wait_marks(self, prod, lo, hi):
        if (prod is self and not prod.is_dma) or hi <= 0:
            return
        assert hi <= prod.n, (
            f"{self.name} waits {prod.name} future mark {hi} > {prod.n}")
        prod._emit_wait(self, lo, hi)

    def _emit_wait(self, waiter, lo, hi):
        key = self.name
        if waiter.waited.get(key, 0) >= hi:
            return
        waiter.e.wait_ge(self.sem, hi)
        waiter.waited[key] = hi

    def selfbar(self):
        if self.n > 0 and getattr(self, "last_sb", -1) < self.n:
            self.e.wait_ge(self.sem, self.n)
            self.last_sb = self.n

    def wstate(self):
        return self.n, dict(self.waited), getattr(self, "last_sb", -1)

    def wrestore(self, st):
        self.n, self.waited = st[0], dict(st[1])
        self.last_sb = st[2]


class _DmaEng:

    K = 8

    def __init__(self, nc, eng, name):
        self.e = eng
        self.name = name
        self.sems = [nc.alloc_semaphore(f"dq_{name}{i}") for i in range(self.K)]
        self.cnt = [0] * self.K
        self.n = 0
        self.waited = {}
        self.is_dma = True

    def mark(self, ins):
        k = self.n % self.K
        if self.cnt[k] > 0:
            ins._wait_ge(self.sems[k], self.cnt[k] * 16)
        ins.then_inc(self.sems[k], 16)
        self.cnt[k] += 1
        self.n += 1
        return ins

    def wait_marks(self, prod, lo, hi):
        if (prod is self and not prod.is_dma) or hi <= 0:
            return
        assert hi <= prod.n, (
            f"{self.name} waits {prod.name} future mark {hi} > {prod.n}")
        prod._emit_wait(self, lo, hi)

    def _emit_wait(self, waiter, lo, hi):
        want = {}
        for m in range(lo, hi + 1):
            k = (m - 1) % self.K
            c = (m - 1) // self.K + 1
            want[k] = max(want.get(k, 0), c)
        for k, c in want.items():
            key = (self.name, k)
            if waiter.waited.get(key, 0) >= c:
                continue
            waiter.e.wait_ge(self.sems[k], c * 16)
            waiter.waited[key] = c

    def drain(self):
        for k in range(self.K):
            if self.cnt[k] > 0:
                self.e.wait_ge(self.sems[k], self.cnt[k] * 16)

    def wstate(self):
        return self.n, dict(self.waited), list(self.cnt)

    def wrestore(self, st):
        self.n, self.waited, self.cnt = st[0], dict(st[1]), list(st[2])


class _SB:
    def __init__(self, engs):
        self.w = {}
        self.r = {}
        self.engs = engs

    def rd(self, eng, *keys):
        for k in keys:
            wr = self.w.get(k)
            if wr is not None:
                if wr[0] is eng and not eng.is_dma:
                    assert wr[2] <= eng.n, (k, wr, eng.name, eng.n)
                    if wr[2] >= eng.n - 2:
                        eng.selfbar()
                else:
                    eng.wait_marks(wr[0], wr[1], wr[2])
            self.r.setdefault(k, []).append((eng, eng.n + 1))

    def wr(self, eng, *keys):
        for k in keys:
            wr = self.w.get(k)
            if wr is not None:
                eng.wait_marks(wr[0], wr[1], wr[2])
            for re, rn in self.r.get(k, ()):
                eng.wait_marks(re, rn, rn)
            self.r[k] = []
            self.w[k] = (eng, eng.n + 1, eng.n + 1)

    def bind_range(self, eng, lo_n, *keys):
        for k in keys:
            self.w[k] = (eng, lo_n + 1, eng.n)

    def barrier(self):
        for a in self.engs:
            for b in self.engs:
                a.wait_marks(b, 1, b.n)

    def state(self):
        return (dict(self.w), {k: list(v) for k, v in self.r.items()},
                {e.name: e.n for e in self.engs},
                {e.name: e.wstate() for e in self.engs})

    def restore(self, st):
        self.w = dict(st[0])
        self.r = {k: list(v) for k, v in st[1].items()}
        for e in self.engs:
            e.wrestore(st[3][e.name])


def _rope_tables():
    invf = (1.0 / 10000.0 ** (np.arange(0, 16, 2) / 16)).astype(np.float64)
    fr = np.outer(np.arange(S, dtype=np.float64), invf)
    ct = np.cos(fr).astype(np.float32).reshape(NU, 128, 8).transpose(1, 0, 2)
    st = np.sin(fr).astype(np.float32).reshape(NU, 128, 8).transpose(1, 0, 2)
    return np.ascontiguousarray(ct), np.ascontiguousarray(st)


def _band_masks():
    i = np.arange(128)[:, None]
    kw = np.arange(143)[None, :]
    band = np.where((kw >= i) & (kw <= i + 15), 0.0, NEG).astype(np.float32)
    band0 = np.where(kw >= 15, band, NEG).astype(np.float32)
    return band, band0


def _diag_mask():
    i = np.arange(128)[:, None]
    k = np.arange(128)[None, :]
    return np.where(k <= i, 0.0, NEG).astype(np.float32)


def build(dump=None):
    nc = bass.Bass("TRN2", target_bir_lowering=False,
                   detect_race_conditions=False)

    xf_h = nc.dram_tensor("xf", [S, D], F32, kind="ExternalInput")
    xs_h = nc.dram_tensor("xs", [4, 512, D], F32, kind="ExternalInput")
    jsel_h = nc.dram_tensor("jsel", [1, 1], I32, kind="ExternalInput")
    wqkv_h = nc.dram_tensor("wqkv", [3 * D, D], F32, kind="ExternalInput")
    bqkv_h = nc.dram_tensor("bqkv", [3 * D], F32, kind="ExternalInput")
    wqa_h = nc.dram_tensor("wqa", [D, D], F32, kind="ExternalInput")
    bqa_h = nc.dram_tensor("bqa", [D], F32, kind="ExternalInput")
    wka_h = nc.dram_tensor("wka", [D, D], F32, kind="ExternalInput")
    bka_h = nc.dram_tensor("bka", [D], F32, kind="ExternalInput")
    wd_h = nc.dram_tensor("wd", [D, D], F32, kind="ExternalInput")
    bd_h = nc.dram_tensor("bd", [D], F32, kind="ExternalInput")
    rw_h = nc.dram_tensor("rw", [D, E], F32, kind="ExternalInput")
    rb_h = nc.dram_tensor("rb", [E], F32, kind="ExternalInput")
    sw1_h = nc.dram_tensor("sw1", [D, H2], F32, kind="ExternalInput")
    sb1_h = nc.dram_tensor("sb1", [H2], F32, kind="ExternalInput")
    sw2_h = nc.dram_tensor("sw2", [H2, D], F32, kind="ExternalInput")
    sb2_h = nc.dram_tensor("sb2", [D], F32, kind="ExternalInput")
    ew1_h = nc.dram_tensor("ew1", [E, D, H2], F32, kind="ExternalInput")
    eb1_h = nc.dram_tensor("eb1", [E, H2], F32, kind="ExternalInput")
    ew2_h = nc.dram_tensor("ew2", [E, H2, D], F32, kind="ExternalInput")
    eb2_h = nc.dram_tensor("eb2", [E, D], F32, kind="ExternalInput")
    out_h = nc.dram_tensor("out", [4, 128, TU, D], F32, kind="ExternalOutput")

    wqkvT_c = nc.dram_tensor("wqkvT_c", [128, 8, 3 * D], BF16, kind="Internal")
    wqaT_c = nc.dram_tensor("wqaT_c", [128, 8, D], BF16, kind="Internal")
    wkaT_c = nc.dram_tensor("wkaT_c", [128, 8, D], BF16, kind="Internal")
    wdT_c = nc.dram_tensor("wdT_c", [128, 8, D], BF16, kind="Internal")
    spill_kind = "ExternalOutput" if dump == "B" else "Internal"
    if dump == "C1":
        xn2a_d = nc.dram_tensor("xn2a_d", [128, 8, 2048], BF16,
                                kind="ExternalOutput")
        wall_d = nc.dram_tensor("wall_d", [128, 16, E], F32,
                                kind="ExternalOutput")
        mix_d = nc.dram_tensor("mix_d", [128, 16, D], F32,
                               kind="ExternalOutput")
        wt9_d = nc.dram_tensor("wt9_d", [9, 2048], BF16,
                               kind="ExternalOutput")
    x2_sp = nc.dram_tensor("x2_sp", [4, 128, TU, D], F32, kind=spill_kind)
    xn2_sp = nc.dram_tensor("xn2_sp", [4, 128, 8, 512], BF16, kind=spill_kind)
    if dump == "A":
        atm_d = nc.dram_tensor("atm_d", [128, NU, NH, 65], BF16,
                               kind="ExternalOutput")
        q_d = nc.dram_tensor("q_d", [128, TU, D], BF16, kind="ExternalOutput")
        k_d = nc.dram_tensor("k_d", [128, TU, D], BF16, kind="ExternalOutput")
        v_d = nc.dram_tensor("v_d", [128, TU, D], BF16, kind="ExternalOutput")
        pb_d = nc.dram_tensor("pb_d", [128, NH, 256], BF16,
                              kind="ExternalOutput")
        ws_d = nc.dram_tensor("ws_d", [128, NH], F32, kind="ExternalOutput")
        xnf_d = nc.dram_tensor("xnf_d", [128, 8, 512], BF16,
                               kind="ExternalOutput")
        st_d = nc.dram_tensor("st_d", [128, 16], F32, kind="ExternalOutput")
        sq_d = nc.dram_tensor("sq_d", [128, D], F32, kind="ExternalOutput")
        wq_d = nc.dram_tensor("wq_d", [128, 8, 3 * D], BF16,
                              kind="ExternalOutput")

    ident_h = nc.inline_tensor(np.eye(128, dtype=ml_dtypes.bfloat16), "identc")
    band_np, band0_np = _band_masks()
    band_h = nc.inline_tensor(band_np, "bandc")
    band0_h = nc.inline_tensor(band0_np, "band0c")
    diag_h = nc.inline_tensor(
        np.ascontiguousarray(_diag_mask().T).astype(ml_dtypes.bfloat16),
        "diagc")
    posug_np = (np.arange(128)[:, None] +
                128.0 * np.arange(NU)[None, :]).astype(np.float32)
    posug_h = nc.inline_tensor(posug_np, "posugc")
    i8_h = nc.inline_tensor(
        np.tile((np.arange(8) * 0.125).astype(np.float32), (128, 1)),
        "i8c")

    T = _Eng(nc, nc.tensor, "T")
    V = _Eng(nc, nc.vector, "V")
    SC = _Eng(nc, nc.scalar, "SC")
    SY = _DmaEng(nc, nc.sync, "SY")
    GP = _DmaEng(nc, nc.gpsimd, "GP")
    ALLE = (T, V, SC, SY, GP)
    sb = _SB(ALLE)

    st_ctx = ExitStack()
    sbuf = lambda n, s, d: st_ctx.enter_context(nc.sbuf_tensor(n, s, d))

    ident = sbuf("ident", [128, 128], BF16)
    ones_bf = sbuf("ones_bf", [1, 128], BF16)
    jsel_sb = sbuf("jsel_sb", [1, 1], I32)
    stats = sbuf("stats", [128, 16], F32)
    sq_sc = sbuf("sq_sc", [128, D], F32)
    epsc = sbuf("epsc", [128, 1], F32)

    sb.wr(SY, "consts")
    _n0 = SY.n
    SY.mark(nc.sync.dma_start(ident[:, :], ident_h[:, :]))
    SY.mark(nc.sync.dma_start(jsel_sb[:, :], jsel_h[:, :]))
    sb.bind_range(SY, _n0, "consts")
    sb.wr(V, "ones")
    nc.vector.memset(epsc[:, :], 1e-5)
    V.mark(nc.vector.memset(ones_bf[:, :], 1.0))
    for e in ALLE:
        sb.rd(e, "consts")
    jreg = nc.values_load(jsel_sb[0:1, 0:1],
                          engines=[e.e.engine for e in ALLE],
                          min_val=0, max_val=1,
                          skip_runtime_bounds_check=True)

    def layer_norm_batch(x_aps, out_aps, skey):
        n = len(x_aps)
        negm = lambda u: stats[:, 2 * u:2 * u + 1]
        rstd = lambda u: stats[:, 2 * u + 1:2 * u + 2]
        ssum = lambda u: stats[:, 8 + u:9 + u]
        stdv = lambda u: stats[:, 12 + u:13 + u]
        for u in range(n):
            sb.rd(V, skey)
            sb.wr(V, ("st_a", u))
            V.mark(nc.vector.reduce_sum(ssum(u), x_aps[u], axis=AX.X))
        V.selfbar()
        for u in range(n):
            sb.rd(V, ("st_a", u))
            sb.wr(V, ("st_b", u))
            V.mark(nc.vector.tensor_scalar_mul(negm(u), ssum(u), -1.0 / D))
        for u in range(n):
            sb.rd(SC, skey, ("st_b", u))
            sb.wr(SC, "sq_sc")
            SC.mark(nc.scalar.activation(sq_sc[:, :], x_aps[u], AF.Square,
                                         bias=negm(u), scale=1.0))
            sb.rd(V, "sq_sc")
            sb.wr(V, ("st_c", u))
            V.mark(nc.vector.reduce_sum(stdv(u), sq_sc[:, :], axis=AX.X))
        for u in range(n):
            sb.rd(SC, ("st_c", u), "ones")
            sb.wr(SC, ("st_d", u))
            SC.mark(nc.scalar.activation(stdv(u), stdv(u), AF.Ln,
                                         bias=epsc[:, :], scale=1.0 / D))
        for u in range(n):
            sb.rd(SC, ("st_d", u))
            sb.wr(SC, ("st_e", u))
            SC.mark(nc.scalar.activation(rstd(u), stdv(u), AF.Exp,
                                         scale=-0.5))
        for u in range(n):
            sb.rd(V, skey, ("st_b", u), ("st_e", u))
            V.mark(nc.vector.tensor_scalar(out_aps[u], x_aps[u], negm(u),
                                           rstd(u), ALU.add, ALU.mult))

    with ExitStack() as p0:
        wrows = p0.enter_context(nc.sbuf_tensor("p0_rows", [128, 24, D], BF16))
        colbuf = p0.enter_context(nc.sbuf_tensor("p0_col", [128, 3 * D], BF16))
        ps_t = p0.enter_context(nc.psum_tensor("p0_pst", [128, 512], BF16))

        for wname, w_h, cache, nrt, scale in (
            ("wqkv", wqkv_h, wqkvT_c, 24, None),
        ):
            src = w_h.ap().rearrange("(r p) d -> p r d", p=128)
            sb.wr(GP, "p0rows")
            GP.mark(nc.gpsimd.dma_start(wrows[:, 0:nrt, :], src))
            for c in range(8):
                for r0 in range(0, nrt, 4):
                    rblk = min(4, nrt - r0)
                    sb.rd(T, "p0rows", "consts")
                    sb.wr(T, "p0ps")
                    ins = None
                    for r in range(r0, r0 + rblk):
                        ins = nc.tensor.transpose(
                            ps_t[:, (r - r0) * 128:(r - r0 + 1) * 128],
                            wrows[:, r, c * 128:(c + 1) * 128], ident[:, :])
                    T.mark(ins)
                    sb.rd(V, "p0ps")
                    sb.wr(V, ("p0cb", r0))
                    V.mark(nc.vector.tensor_copy(
                        colbuf[:, r0 * 128:(r0 + rblk) * 128],
                        ps_t[:, 0:rblk * 128]))
                sb.rd(SY, *[("p0cb", r0) for r0 in range(0, nrt, 4)])
                sb.wr(SY, (wname + "T_c", c))
                SY.mark(nc.sync.dma_start(cache.ap()[:, c, 0:nrt * 128],
                                          colbuf[:, 0:nrt * 128]))
        sb.barrier()

    with ExitStack() as pab:
        a_tm = pab.enter_context(nc.sbuf_tensor("a_tm", [128, NU, NH, 65],
                                                BF16))
        sb.wr(V, "aones")
        V.mark(nc.vector.memset(
            a_tm[:, :, :, 64:65].rearrange("p a b c -> p (a b c)"), 1.0))

        with ExitStack() as pa:
            sbA = lambda n, s, d: pa.enter_context(nc.sbuf_tensor(n, s, d))
            psA = lambda n, s, d=F32: pa.enter_context(nc.psum_tensor(n, s, d))
            wqkvT = sbA("wqkvT", [128, 8, 3 * D], BF16)
            x_tm = sbA("x_tm", [128, TU, D], F32)
            q_tm = sbA("q_tm", [128, TU, D], BF16)
            xn_tm = q_tm
            xn_fm = sbA("xn_fm", [128, 8, 512], BF16)
            k_tm = sbA("k_tm", [128, TU, D], BF16)
            v_tm = sbA("v_tm", [128, TU, D], BF16)
            q_fm = sbA("q_fm", [128, 8, 512], BF16)
            k_fm = sbA("k_fm", [128, 8, 512], BF16)
            k_last = sbA("k_last", [128, 8, 15], BF16)
            v_last = sbA("v_last", [128, D], BF16)
            pbuf = sbA("pbuf", [128, NH, 256], BF16)
            pt_sb = [sbA("pt_sb%d" % i, [128, 256], BF16) for i in range(2)]
            wsum = sbA("wsum", [128, NH], F32)
            wrec = sbA("wrec", [128, NH], F32)
            rsc = sbA("rsc", [128, 2, 256], BF16)
            band = sbA("band", [128, 143], F32)
            band0 = sbA("band0", [128, 143], F32)
            ct8 = sbA("ct8", [128, NU, 8], F32)
            st8 = sbA("st8", [128, NU, 8], F32)
            bqkv_row = sbA("bqkv_row", [1, 3 * D], BF16)
            wsm = [sbA("wsm%d" % i, [128, 4, 128], BF16) for i in range(2)]
            rtmp = x_tm[:, 0, 0:256]
            posug = x_tm[:, 1, 0:NU]
            i8s = x_tm[:, 1, 64:72]
            rlad = x_tm[:, 1, 128:141]
            cwb = [sbA("cwb%d" % i, [128, 512], BF16) for i in range(2)]
            ps_mm = psA("ps_mm", [128, 1024])
            ps_tp = psA("ps_tp", [128, 2048], BF16)
            ps_w = psA("ps_w", [128, 1024])
            ps_a = psA("ps_a", [128, 1024])

            sb.wr(SY, "constsA")
            _n0 = SY.n
            SY.mark(nc.sync.dma_start(band[:, :], band_h[:, :]))
            SY.mark(nc.sync.dma_start(band0[:, :], band0_h[:, :]))
            SY.mark(nc.sync.dma_start(posug[:, :], posug_h[:, :]))
            SY.mark(nc.sync.dma_start(i8s[:, :], i8_h[:, :]))
            sb.bind_range(SY, _n0, "constsA")
            sb.wr(SY, "sq_sc")
            SY.mark(nc.sync.dma_start(
                sq_sc[0:1, 0:D], bqkv_h.ap()[0:D].rearrange("(a f) -> a f",
                                                            a=1)))
            sb.wr(GP, "constsA_gp")
            GP.mark(nc.gpsimd.dma_start(
                bqkv_row[0:1, D:3 * D],
                bqkv_h.ap()[D:3 * D].rearrange("(a f) -> a f", a=1)))
            sb.rd(SC, "sq_sc")
            sb.wr(SC, "constsA2")
            SC.mark(nc.scalar.activation(bqkv_row[0:1, 0:D], sq_sc[0:1, 0:D],
                                         AF.Copy, scale=1.0))
            sb.rd(SY, *[("wqkvT_c", c) for c in range(8)])
            sb.wr(SY, "wqkvT")
            SY.mark(nc.sync.dma_start(wqkvT[:, :, :], wqkvT_c.ap()[:, :, :]))
            sb.wr(V, "klast", "vlast")
            nc.vector.memset(k_last[:, :, :].rearrange("p a b -> p (a b)"), 0.0)
            nc.vector.memset(v_last[:, :], 0.0)
            sb.wr(V, "pbufz")
            V.mark(nc.vector.memset(
                pbuf[:, :, :].rearrange("p a b -> p (a b)"), 0.0))

            ct8f = ct8[:, :, :].rearrange("p a b -> p (a b)")
            st8f = st8[:, :, :].rearrange("p a b -> p (a b)")
            sb.rd(V, "constsA")
            sb.wr(V, "rop0")
            V.mark(nc.vector.memset(rlad[:, 0:1], 1.0))
            sb.rd(SC, "rop0")
            sb.wr(SC, "rop1")
            SC.mark(nc.scalar.activation(rlad[:, 0:1], rlad[:, 0:1],
                                         AF.Arctan))
            sb.rd(V, "rop1")
            sb.wr(V, "rop2")
            V.mark(nc.vector.tensor_scalar_mul(rlad[:, 1:2], rlad[:, 0:1],
                                               4.0))
            sb.rd(V, "rop2")
            sb.wr(V, "rop2")
            V.mark(nc.vector.tensor_scalar_mul(rlad[:, 0:1], rlad[:, 0:1],
                                               2.0))
            for j in range(10):
                sb.rd(V, "rop2")
                sb.wr(V, "rop2")
                V.mark(nc.vector.tensor_scalar_mul(
                    rlad[:, 2 + j:3 + j], rlad[:, 1 + j:2 + j], 2.0))
            sb.wr(V, "rop3")
            V.mark(nc.vector.memset(rlad[:, 12:13], 10000.0))
            sb.rd(SC, "rop3")
            sb.wr(SC, "rop4")
            SC.mark(nc.scalar.activation(rlad[:, 12:13], rlad[:, 12:13],
                                         AF.Ln))
            sb.rd(V, "rop4", "constsA")
            sb.wr(V, "rop5")
            V.mark(nc.vector.tensor_scalar(i8s[:, :], i8s[:, :],
                                           rlad[:, 12:13], None, ALU.mult))
            sb.rd(SC, "rop5")
            sb.wr(SC, "rop6")
            SC.mark(nc.scalar.activation(i8s[:, :], i8s[:, :], AF.Exp,
                                         scale=-1.0))
            for u in range(NU):
                sb.rd(V, "rop6", "constsA")
                sb.wr(V, ("ang", u))
                V.mark(nc.vector.tensor_scalar(
                    ct8[:, u, :], i8s[:, :], posug[:, u:u + 1], None,
                    ALU.mult))
            sb.w["ang2"] = (V, V.n, V.n)
            for j in range(9, -1, -1):
                cj = rlad[:, 2 + j:3 + j]
                sb.rd(V, "ang2")
                sb.wr(V, "rsel")
                V.mark(nc.vector.tensor_scalar(rtmp[:, :], ct8f, cj, cj,
                                               ALU.is_ge, ALU.mult))
                sb.rd(V, "rsel")
                sb.wr(V, "ang2")
                V.mark(nc.vector.tensor_sub(ct8f, ct8f, rtmp[:, :]))
            sb.rd(V, "ang2")
            sb.wr(V, "rsel")
            V.mark(nc.vector.tensor_scalar(rtmp[:, :], ct8f, rlad[:, 1:2],
                                           rlad[:, 2:3], ALU.is_ge,
                                           ALU.mult))
            sb.rd(V, "rsel")
            sb.wr(V, "sarg")
            V.mark(nc.vector.tensor_sub(st8f, ct8f, rtmp[:, :]))
            sb.rd(V, "ang2")
            sb.wr(V, "carg")
            V.mark(nc.vector.tensor_scalar(rtmp[:, :], ct8f, rlad[:, 0:1],
                                           None, ALU.add))
            for pcol in (2, 1):
                sb.rd(V, "carg")
                sb.wr(V, "csel")
                V.mark(nc.vector.tensor_scalar(ct8f, rtmp[:, :],
                                               rlad[:, pcol:pcol + 1],
                                               rlad[:, 2:3], ALU.is_ge,
                                               ALU.mult))
                sb.rd(V, "csel")
                sb.wr(V, "carg")
                V.mark(nc.vector.tensor_sub(rtmp[:, :], rtmp[:, :], ct8f))
            sb.rd(SC, "sarg")
            sb.wr(SC, "ropetab")
            SC.mark(nc.scalar.activation(st8f, st8f, AF.Sin))
            sb.rd(SC, "carg")
            sb.wr(SC, "ropetab")
            SC.mark(nc.scalar.activation(ct8f, rtmp[:, :], AF.Sin))

            cw_units = [(wname, w_h, cache, c, r0)
                        for wname, w_h, cache in (
                            ("wka", wka_h, wkaT_c),
                            ("wqa", wqa_h, wqaT_c),
                            ("wd", wd_h, wdT_c))
                        for c in range(8) for r0 in (0, 4)]
            cw_ctr = [0]

            def emit_cache_unit():
                wname, w_h, cache, c, r0 = cw_units.pop(0)
                ci = cw_ctr[0] % 2
                cw_ctr[0] += 1
                sb.wr(GP, ("cwsm", ci))
                GP.mark(nc.gpsimd.dma_start(
                    wsm[ci][:, :, :],
                    w_h.ap().rearrange("(r p) d -> p r d", p=128)[
                        :, r0:r0 + 4, c * 128:(c + 1) * 128]))
                slot = ci
                tb = slot * 1024
                sb.rd(T, ("cwsm", ci), "consts")
                sb.wr(T, ("ps_tp", slot))
                ins = None
                for r in range(4):
                    ins = nc.tensor.transpose(
                        ps_tp[:, tb + r * 128:tb + (r + 1) * 128],
                        wsm[ci][:, r, :], ident[:, :])
                T.mark(ins)
                ceng = V if ci == 0 else SC
                sb.rd(ceng, ("ps_tp", slot))
                sb.wr(ceng, ("cwb", ci))
                if ceng is V:
                    ceng.mark(nc.vector.tensor_copy(
                        cwb[ci][:, :], ps_tp[:, tb:tb + 512]))
                else:
                    ceng.mark(nc.scalar.copy(
                        cwb[ci][:, :], ps_tp[:, tb:tb + 512]))
                sb.rd(SY, ("cwb", ci))
                sb.wr(SY, (wname + "T_c", c, r0))
                SY.mark(nc.sync.dma_start(
                    cache.ap()[:, c, r0 * 128:(r0 + 4) * 128],
                    cwb[ci][:, :]))

            for g in range(NT):
                sb.rd(SY, "ropetab")
                sb.wr(SY, "x_tm")
                SY.mark(nc.sync.dma_start(
                    x_tm[:, :, :],
                    xf_h.ap()[g * 512:(g + 1) * 512, :].rearrange(
                        "(u p) d -> p u d", p=128)))
                layer_norm_batch([x_tm[:, u, :] for u in range(TU)],
                                 [xn_tm[:, u, :] for u in range(TU)], "x_tm")
                for u in range(TU):
                    mk = V.n - (TU - 1 - u)
                    sb.w[("q", u, 0)] = (V, mk, mk)
                    sb.w[("q", u, 1)] = (V, mk, mk)
                for c in range(8):
                    slot = c % 2
                    tb = slot * 1024
                    sb.rd(T, "consts", *[("q", u, 0 if c < 4 else 1)
                                         for u in range(TU)])
                    sb.wr(T, ("ps_tp", slot))
                    ins = None
                    for u in range(TU):
                        ins = nc.tensor.transpose(
                            ps_tp[:, tb + u * 128:tb + (u + 1) * 128],
                            xn_tm[:, u, c * 128:(c + 1) * 128], ident[:, :])
                    T.mark(ins)
                    ceng = V if c % 2 == 0 else SC
                    sb.rd(ceng, ("ps_tp", slot))
                    sb.wr(ceng, ("xn_fm", c))
                    if ceng is V:
                        ceng.mark(nc.vector.tensor_copy(
                            xn_fm[:, c, :], ps_tp[:, tb:tb + 512]))
                    else:
                        ceng.mark(nc.scalar.copy(
                            xn_fm[:, c, :], ps_tp[:, tb:tb + 512]))
                for u in range(TU):
                    for oc in range(6):
                        slot = (u * 6 + oc) % 2
                        sb.rd(T, "wqkvT", "constsA2", "constsA_gp", "ones",
                              *[("xn_fm", c) for c in range(8)])
                        sb.wr(T, ("ps_mm", slot))
                        for c in range(8):
                            nc.tensor.matmul(
                                ps_mm[:, slot * 512:(slot + 1) * 512],
                                xn_fm[:, c, u * 128:(u + 1) * 128],
                                wqkvT[:, c, oc * 512:(oc + 1) * 512],
                                start=(c == 0), stop=False)
                        T.mark(nc.tensor.matmul(
                            ps_mm[:, slot * 512:(slot + 1) * 512],
                            ones_bf[0:1, :],
                            bqkv_row[0:1, oc * 512:(oc + 1) * 512],
                            start=False, stop=True))
                        dst = (q_tm if oc < 2 else k_tm if oc < 4 else v_tm)
                        dcol = (oc % 2) * 512
                        okey = ("q", u, oc) if oc < 4 else ("v", u, oc)
                        sb.rd(SC, ("ps_mm", slot))
                        sb.wr(SC, okey)
                        SC.mark(nc.scalar.copy(
                            dst[:, u, dcol:dcol + 512],
                            ps_mm[:, slot * 512:(slot + 1) * 512]))
                ug0 = g * TU
                for u in range(TU):
                    for t_tm, oc0 in ((q_tm, 0), (k_tm, 2)):
                        sb.rd(V, ("q", u, oc0), ("q", u, oc0 + 1),
                              "constsA", "ropetab")
                        sb.wr(V, "rsc")
                        hd = t_tm[:, u, :].rearrange("p (h d) -> p h d", d=HD)
                        lo, hi = hd[:, :, 0:8], hd[:, :, 8:16]
                        ctu = ct8[:, ug0 + u:ug0 + u + 1, :]
                        stu = st8[:, ug0 + u:ug0 + u + 1, :]
                        r2 = rsc[:, 0, :].rearrange("p (h d) -> p h d", d=16)
                        r3 = rsc[:, 1, :].rearrange("p (h d) -> p h d", d=16)
                        rlo, tlo = r2[:, :, 0:8], r2[:, :, 8:16]
                        rhi, thi = r3[:, :, 0:8], r3[:, :, 8:16]
                        cb, _ = bass.broadcast_tensor_aps(ctu, lo)
                        sbr, _ = bass.broadcast_tensor_aps(stu, lo)
                        nc.vector.tensor_mul(rlo, lo, cb)
                        nc.vector.tensor_mul(tlo, hi, sbr)
                        nc.vector.tensor_mul(rhi, hi, cb)
                        V.mark(nc.vector.tensor_mul(thi, lo, sbr))
                        sb.rd(V, "rsc")
                        sb.wr(V, ("q", u, oc0), ("q", u, oc0 + 1))
                        nc.vector.tensor_sub(lo, rlo, tlo)
                        V.mark(nc.vector.tensor_add(hi, rhi, thi))
                for t_tm, t_fm, tag, oc0 in ((q_tm, q_fm, "qfm", 0),
                                             (k_tm, k_fm, "kfm", 2)):
                    for c in range(8):
                        slot = c % 2
                        tb = slot * 1024
                        sb.rd(T, "consts", *[("q", u, oc0 + (1 if c >= 4 else 0))
                                             for u in range(TU)])
                        sb.wr(T, ("ps_tp", slot))
                        ins = None
                        for u in range(TU):
                            ins = nc.tensor.transpose(
                                ps_tp[:, tb + u * 128:tb + (u + 1) * 128],
                                t_tm[:, u, c * 128:(c + 1) * 128], ident[:, :])
                        T.mark(ins)
                        ceng = V if c % 2 == 0 else SC
                        sb.rd(ceng, ("ps_tp", slot))
                        sb.wr(ceng, (tag, c))
                        if ceng is V:
                            ceng.mark(nc.vector.tensor_copy(
                                t_fm[:, c, :], ps_tp[:, tb:tb + 512]))
                        else:
                            ceng.mark(nc.scalar.copy(
                                t_fm[:, c, :], ps_tp[:, tb:tb + 512]))
                for u in range(TU):
                    for h in range(NH):
                        ro = (h % 2) * 64
                        hc = h // 2
                        slot = h % 2
                        lhs = q_fm[ro:ro + 64, hc, u * 128:(u + 1) * 128]
                        sb.rd(T, ("qfm", hc), ("kfm", hc))
                        if u == 0:
                            sb.rd(T, "klast")
                        sb.wr(T, ("ps_w", slot))
                        if u == 0:
                            nc.tensor.matmul(
                                ps_w[:, slot * 512:slot * 512 + 15], lhs,
                                k_last[ro:ro + 64, hc, :],
                                start=True, stop=True)
                            T.mark(nc.tensor.matmul(
                                ps_w[:, slot * 512 + 15:slot * 512 + 143],
                                lhs, k_fm[ro:ro + 64, hc, 0:128],
                                start=True, stop=True))
                        else:
                            T.mark(nc.tensor.matmul(
                                ps_w[:, slot * 512:slot * 512 + 143], lhs,
                                k_fm[ro:ro + 64, hc, u * 128 - 15:u * 128 + 128],
                                start=True, stop=True))
                        bnd = band0 if (g == 0 and u == 0) else band
                        sb.rd(V, ("ps_w", slot), "constsA")
                        sb.wr(V, ("ps_w", slot))
                        V.mark(nc.vector.tensor_add(
                            ps_w[:, slot * 512:slot * 512 + 143],
                            ps_w[:, slot * 512:slot * 512 + 143], bnd[:, :]))
                        sb.rd(SC, ("ps_w", slot), "pbufz")
                        sb.wr(SC, ("pbuf", h), ("wsum", h))
                        SC.mark(nc.scalar.activation(
                            pbuf[:, h, 113:256],
                            ps_w[:, slot * 512:slot * 512 + 143],
                            AF.Exp, scale=0.125,
                            accum_out=wsum[:, h:h + 1]))
                    sb.rd(V, *[("wsum", h) for h in range(NH)])
                    sb.wr(V, "wrec")
                    V.mark(nc.vector.reciprocal(wrec[:, :], wsum[:, :]))
                    V.selfbar()
                    def w_pt(h):
                        slot = h % 2
                        tb = slot * 1024
                        sb.rd(T, ("pbuf", h), "consts")
                        sb.wr(T, ("ps_tp", slot))
                        nc.tensor.transpose(ps_tp[:, tb:tb + 128],
                                            pbuf[:, h, 0:128], ident[:, :])
                        T.mark(nc.tensor.transpose(
                            ps_tp[:, tb + 128:tb + 256],
                            pbuf[:, h, 128:256], ident[:, :]))
                        ceng = V if h % 2 == 0 else SC
                        sb.rd(ceng, ("ps_tp", slot))
                        sb.wr(ceng, ("pt_sb", slot))
                        if ceng is V:
                            ceng.mark(nc.vector.tensor_copy(
                                pt_sb[slot][:, 0:256], ps_tp[:, tb:tb + 256]))
                        else:
                            ceng.mark(nc.scalar.copy(
                                pt_sb[slot][:, 0:256], ps_tp[:, tb:tb + 256]))

                    def w_pv(h):
                        slot = h % 2
                        vkey = ("v", u, 4 + (1 if h >= 8 else 0))
                        if u > 0:
                            pv_rhs1 = v_tm[:, u - 1, h * 64:h * 64 + 64]
                            pkey = ("v", u - 1, 4 + (1 if h >= 8 else 0))
                        else:
                            pv_rhs1 = v_last[:, h * 64:h * 64 + 64]
                            pkey = "vlast"
                        sb.rd(T, ("pt_sb", slot), vkey, pkey)
                        sb.wr(T, ("ps_a", slot))
                        nc.tensor.matmul(ps_a[:, slot * 512:slot * 512 + 64],
                                         pt_sb[slot][:, 0:128], pv_rhs1,
                                         start=True, stop=False)
                        T.mark(nc.tensor.matmul(
                            ps_a[:, slot * 512:slot * 512 + 64],
                            pt_sb[slot][:, 128:256],
                            v_tm[:, u, h * 64:h * 64 + 64],
                            start=False, stop=True))
                        sb.rd(V, ("ps_a", slot), "wrec")
                        sb.wr(V, ("a_tm", ug0 + u))
                        V.mark(nc.vector.tensor_scalar(
                            a_tm[:, ug0 + u, h, 0:64],
                            ps_a[:, slot * 512:slot * 512 + 64],
                            wrec[:, h:h + 1], None, ALU.mult))

                    for h in range(NH):
                        w_pt(h)
                        if h > 0:
                            w_pv(h - 1)
                    w_pv(NH - 1)
                sb.rd(V, ("kfm", 0))
                for c in range(8):
                    sb.rd(V, ("kfm", c))
                sb.wr(V, "klast")
                V.mark(nc.vector.tensor_copy(k_last[:, :, :],
                                              k_fm[:, :, 497:512]))
                sb.rd(V, ("v", TU - 1, 4), ("v", TU - 1, 5))
                sb.wr(V, "vlast")
                V.mark(nc.vector.tensor_copy(v_last[:, :], v_tm[:, TU - 1, :]))
                for _ in range(6):
                    if cw_units:
                        emit_cache_unit()

            if dump == "A":
                sb.rd(SY, *[("a_tm", uu) for uu in range(NU)])
                SY.mark(nc.sync.dma_start(atm_d.ap()[:, :, :, :],
                                          a_tm[:, :, :, :]))
                sb.barrier()
                SY.mark(nc.sync.dma_start(q_d.ap()[:, :, :], q_tm[:, :, :]))
                SY.mark(nc.sync.dma_start(k_d.ap()[:, :, :], k_tm[:, :, :]))
                SY.mark(nc.sync.dma_start(v_d.ap()[:, :, :], v_tm[:, :, :]))
                SY.mark(nc.sync.dma_start(pb_d.ap()[:, :, :], pbuf[:, :, :]))
                SY.mark(nc.sync.dma_start(ws_d.ap()[:, :], wsum[:, :]))
                SY.mark(nc.sync.dma_start(xnf_d.ap()[:, :, :], xn_fm[:, :, :]))
                SY.mark(nc.sync.dma_start(wq_d.ap()[:, :, :], wqkvT[:, :, :]))
                SY.mark(nc.sync.dma_start(st_d.ap()[:, :], stats[:, :]))
                SY.mark(nc.sync.dma_start(sq_d.ap()[:, :], sq_sc[:, :]))
                SY.drain()
                GP.drain()
                return nc
            sb.barrier()

        with ExitStack() as pb:
            sbB = lambda n, s, d: pb.enter_context(nc.sbuf_tensor(n, s, d))
            psB = lambda n, s, d=F32: pb.enter_context(nc.psum_tensor(n, s, d))
            ka_fm = sbB("ka_fm", [128, 8, S], BF16)
            warena = [sbB("warena%d" % i, [128, 8, D], BF16) for i in range(2)]
            a_fm = sbB("a_fm", [128, 8, 512], BF16)
            qa_fm = sbB("qa_fm", [128, 8, 512], BF16)
            p_sb = sbB("p_sb", [128, 2048], BF16)
            out_tm = sbB("out_tm", [128, TU, D], BF16)
            out_fm = sbB("out_fm", [128, 8, 128], BF16)
            x_u = sbB("x_u", [128, D], F32)
            xn2_tm = sbB("xn2_tm", [128, D], BF16)
            xn2_fmu = sbB("xn2_fmu", [128, 8, 128], BF16)
            s_run = sbB("s_run", [128, TU * NH], F32)
            orc = sbB("orc", [128, NH], F32)
            diagm = sbB("diagm", [128, 128], BF16)
            bqa_col = sbB("bqa_col", [128, 8], F32)
            bka_col = sbB("bka_col", [128, 8], F32)
            bd_row = sbB("bd_row", [1, D], BF16)
            ps_sc = psB("ps_sc", [128, 2048])
            ps_pv = psB("ps_pv", [128, 1024])
            ps_tpB = psB("ps_tpB", [128, 2048], BF16)
            ps_y = ps_sc

            sb.wr(SY, "constsB")
            SY.mark(nc.sync.dma_start(diagm[:, :], diag_h[:, :]))
            sb.wr(GP, "constsB2")
            _n0 = GP.n
            GP.mark(nc.gpsimd.dma_start(
                bd_row[:, :], bd_h.ap().rearrange("(a d) -> a d", a=1)))
            with nc.allow_non_contiguous_dma(reason="tiny col packs, once"):
                GP.mark(nc.gpsimd.dma_start(
                    bqa_col[:, :], bqa_h.ap().rearrange("(c p) -> p c", p=128)))
                GP.mark(nc.gpsimd.dma_start(
                    bka_col[:, :], bka_h.ap().rearrange("(c p) -> p c", p=128)))
            sb.bind_range(GP, _n0, "constsB2")
            sb.rd(V, "constsB2")
            sb.wr(V, "constsB3")
            V.mark(nc.vector.tensor_scalar_mul(bqa_col[:, :], bqa_col[:, :],
                                               0.125))
            sb.rd(SY, *[("wkaT_c", c, r0)
                        for c in range(8) for r0 in (0, 4)])
            sb.wr(SY, ("warena", 0))
            SY.mark(nc.sync.dma_start(warena[0][:, :, :],
                                      wkaT_c.ap()[:, :, :]))

            def emit_phase_b(j):
                mine = MINE[j]
                ka_ai, qa_ai, wd_ai = 0, 1, 1

                def load_arena(cache, key):
                    sb.rd(SY, *[(key + "T_c", c, r0)
                                for c in range(8) for r0 in (0, 4)])
                    sb.wr(SY, ("warena", 1))
                    SY.mark(nc.sync.dma_start(warena[1][:, :, :],
                                              cache.ap()[:, :, :]))

                for g in range(NT):
                    if g in mine:
                        load_arena(wqaT_c, "wqa")
                    for hc in range(8):
                        slot = hc % 2
                        tb = slot * 1024
                        sb.rd(T, "consts", *[("a_tm", g * TU + u)
                                             for u in range(TU)])
                        sb.wr(T, ("ps_tpB", slot))
                        ins = None
                        for u in range(TU):
                            for hh in range(2):
                                ins = nc.tensor.transpose(
                                    ps_tpB[hh * 64:hh * 64 + 64,
                                           tb + u * 128:
                                           tb + (u + 1) * 128],
                                    a_tm[:, g * TU + u, hc * 2 + hh, 0:64],
                                    ident[:, :])
                        T.mark(ins)
                        ceng = V if slot == 0 else SC
                        sb.rd(ceng, ("ps_tpB", slot))
                        sb.wr(ceng, ("a_fm", hc))
                        if ceng is V:
                            ceng.mark(nc.vector.tensor_copy(
                                a_fm[:, hc, :], ps_tpB[:, tb:tb + 512]))
                        else:
                            ceng.mark(nc.scalar.copy(
                                a_fm[:, hc, :], ps_tpB[:, tb:tb + 512]))
                    projs = [("ka", ka_ai, bka_col)]
                    if g in mine:
                        projs.append(("qa", qa_ai, bqa_col))
                    for pname, ai, bcol in projs:
                        for oc in range(8):
                            slot = oc % 2
                            sb.rd(T, ("warena", ai),
                                  *[("a_fm", c) for c in range(8)])
                            sb.wr(T, ("ps_sc", slot))
                            ins = None
                            for c in range(8):
                                ins = nc.tensor.matmul(
                                    ps_y[:, slot * 512:(slot + 1) * 512],
                                    warena[ai][:, c, oc * 128:(oc + 1) * 128],
                                    a_fm[:, c, :], start=(c == 0),
                                    stop=(c == 7))
                            T.mark(ins)
                            sb.rd(V, ("ps_sc", slot), "constsB2", "constsB3")
                            if pname == "ka":
                                dst = ka_fm[:, oc, g * 512:(g + 1) * 512]
                                sb.wr(V, ("ka", g, oc))
                            else:
                                dst = qa_fm[:, oc, :]
                                sb.wr(V, ("qa", oc))
                            if pname == "qa":
                                V.mark(nc.vector.tensor_scalar(
                                    dst,
                                    ps_y[:, slot * 512:(slot + 1) * 512],
                                    0.125, bcol[:, oc:oc + 1],
                                    ALU.mult, ALU.add))
                            else:
                                V.mark(nc.vector.tensor_scalar(
                                    dst,
                                    ps_y[:, slot * 512:(slot + 1) * 512],
                                    bcol[:, oc:oc + 1], None, ALU.add))
                    if g not in mine:
                        continue
                    lt = mine.index(g)
                    K4 = 4 * g + 4
                    chunks = [(h, kc) for h in range(NH)
                              for kc in range(K4)]
                    cslot = {}
                    sctr = [0]

                    def emit_qk(ch):
                        h, kc = ch
                        ro = (h % 2) * 64
                        hc = h // 2
                        slot = sctr[0] % 4
                        sctr[0] += 1
                        cslot[ch] = slot
                        q0 = max(0, kc - 4 * g) * 128
                        sb.rd(T, ("qa", hc), ("ka", kc // 4, hc))
                        sb.wr(T, ("ps_sc", slot))
                        T.mark(nc.tensor.matmul(
                            ps_sc[:, slot * 512 + q0:(slot + 1) * 512],
                            ka_fm[ro:ro + 64, hc, kc * 128:(kc + 1) * 128],
                            qa_fm[ro:ro + 64, hc, q0:512],
                            start=True, stop=True))
                        if kc >= 4 * g:
                            sb.rd(V, ("ps_sc", slot), "constsB")
                            sb.wr(V, ("ps_sc", slot))
                            V.mark(nc.vector.tensor_add(
                                ps_sc[:, slot * 512 + q0:
                                      slot * 512 + q0 + 128],
                                ps_sc[:, slot * 512 + q0:
                                      slot * 512 + q0 + 128],
                                diagm[:, :]))

                    def emit_exp(ch):
                        h, kc = ch
                        slot = cslot[ch]
                        q0 = max(0, kc - 4 * g) * 128
                        if q0 > 0:
                            sb.wr(V, ("p_sbz", slot))
                            V.mark(nc.vector.memset(
                                p_sb[:, slot * 512:slot * 512 + q0], 0.0))
                        sb.rd(SC, ("ps_sc", slot))
                        sb.wr(SC, ("p_sb", slot))
                        SC.mark(nc.scalar.activation(
                            p_sb[:, slot * 512 + q0:(slot + 1) * 512],
                            ps_sc[:, slot * 512 + q0:(slot + 1) * 512],
                            AF.Exp))

                    def emit_pv(ch):
                        h, kc = ch
                        slot = cslot[ch]
                        pvs = (h % 2) * 512
                        sb.rd(T, ("p_sb", slot), ("p_sbz", slot),
                              ("a_tm", kc), "aones")
                        sb.wr(T, ("ps_pv", h % 2))
                        ins = None
                        for uu in range(TU):
                            ins = nc.tensor.matmul(
                                ps_pv[:, pvs + uu * 128:
                                      pvs + uu * 128 + 65],
                                p_sb[:, slot * 512 + uu * 128:
                                     slot * 512 + (uu + 1) * 128],
                                a_tm[:, kc, h, 0:65],
                                start=(kc == 0 and uu == 0),
                                stop=(kc == K4 - 1 and uu == TU - 1))
                        T.mark(ins)

                    def emit_out(h):
                        for uu in range(TU):
                            pvs = (h % 2) * 512 + uu * 128
                            sb.rd(V, ("ps_pv", h % 2))
                            sb.wr(V, ("s_run", uu, h))
                            V.mark(nc.vector.tensor_copy(
                                s_run[:, uu * NH + h:uu * NH + h + 1],
                                ps_pv[:, pvs + 64:pvs + 65]))
                            sb.rd(V, ("ps_pv", h % 2))
                            sb.wr(V, ("out_tm", uu, h))
                            V.mark(nc.vector.tensor_copy(
                                out_tm[:, uu, h * 64:h * 64 + 64],
                                ps_pv[:, pvs:pvs + 64]))

                    emit_qk(chunks[0])
                    emit_qk(chunks[1])
                    if len(chunks) > 2:
                        emit_qk(chunks[2])
                    for i, ch in enumerate(chunks):
                        emit_exp(ch)
                        if i + 3 < len(chunks):
                            emit_qk(chunks[i + 3])
                        emit_pv(ch)
                        if ch[1] == K4 - 1:
                            emit_out(ch[0])
                    load_arena(wdT_c, "wd")
                    for u in range(TU):
                        sb.rd(V, *[("s_run", u, h) for h in range(NH)])
                        sb.wr(V, "orc")
                        V.mark(nc.vector.reciprocal(
                            orc[:, :], s_run[:, u * NH:(u + 1) * NH]))
                        for h in range(NH):
                            sb.rd(V, "orc", ("out_tm", u, h))
                            sb.wr(V, ("out_tm", u, h))
                            V.mark(nc.vector.tensor_scalar(
                                out_tm[:, u, h * 64:h * 64 + 64],
                                out_tm[:, u, h * 64:h * 64 + 64],
                                orc[:, h:h + 1], None, ALU.mult))
                        for c4 in range(2):
                            sb.rd(T, "consts",
                                  *[("out_tm", u, h) for h in range(NH)])
                            sb.wr(T, ("ps_tpB", c4))
                            ins = None
                            for i4 in range(4):
                                c = c4 * 4 + i4
                                ins = nc.tensor.transpose(
                                    ps_tpB[:, c4 * 1024 + i4 * 128:
                                           c4 * 1024 + (i4 + 1) * 128],
                                    out_tm[:, u, c * 128:(c + 1) * 128],
                                    ident[:, :])
                            T.mark(ins)
                            sb.rd(V, ("ps_tpB", c4))
                            sb.wr(V, ("out_fm", c4))
                            V.mark(nc.vector.tensor_copy(
                                out_fm[:, c4 * 4:(c4 + 1) * 4, :].rearrange(
                                    "p a b -> p (a b)"),
                                ps_tpB[:, c4 * 1024:c4 * 1024 + 512]))
                        for half in range(2):
                            sb.rd(T, ("warena", wd_ai), "constsB2", "ones",
                                  ("out_fm", 0), ("out_fm", 1))
                            sb.wr(T, ("ps_sc", half))
                            for c in range(8):
                                nc.tensor.matmul(
                                    ps_y[:, half * 512:(half + 1) * 512],
                                    out_fm[:, c, :],
                                    warena[wd_ai][:, c,
                                                  half * 512:(half + 1) * 512],
                                    start=(c == 0), stop=False)
                            T.mark(nc.tensor.matmul(
                                ps_y[:, half * 512:(half + 1) * 512],
                                ones_bf[0:1, :],
                                bd_row[0:1, half * 512:(half + 1) * 512],
                                start=False, stop=True))
                        sb.wr(SY, "x_u")
                        SY.mark(nc.sync.dma_start(
                            x_u[:, :], xs_h.ap()[lt, u * 128:(u + 1) * 128, :]))
                        sb.rd(V, "x_u", ("ps_sc", 0), ("ps_sc", 1))
                        sb.wr(V, "x_u")
                        V.mark(nc.vector.tensor_add(x_u[:, :], x_u[:, :],
                                                    ps_y[:, 0:1024]))
                        sb.rd(GP, "x_u")
                        sb.wr(GP, ("x2_sp", lt, u))
                        GP.mark(nc.gpsimd.dma_start(x2_sp.ap()[lt, :, u, :],
                                                    x_u[:, :]))
                        layer_norm_batch([x_u[:, :]], [xn2_tm[:, :]],
                                         "x_u")
                        sb.w["xn2_tm"] = (V, V.n, V.n)
                        for c4 in range(2):
                            sb.rd(T, "consts", "xn2_tm")
                            sb.wr(T, ("ps_tpB", c4))
                            ins = None
                            for i4 in range(4):
                                c = c4 * 4 + i4
                                ins = nc.tensor.transpose(
                                    ps_tpB[:, c4 * 1024 + i4 * 128:
                                           c4 * 1024 + (i4 + 1) * 128],
                                    xn2_tm[:, c * 128:(c + 1) * 128],
                                    ident[:, :])
                            T.mark(ins)
                            sb.rd(V, ("ps_tpB", c4))
                            sb.wr(V, ("xn2f", c4))
                            ins = None
                            for i4 in range(4):
                                c = c4 * 4 + i4
                                ins = nc.vector.tensor_copy(
                                    xn2_fmu[:, c, :],
                                    ps_tpB[:, c4 * 1024 + i4 * 128:
                                           c4 * 1024 + (i4 + 1) * 128])
                            V.mark(ins)
                        sb.rd(GP, ("xn2f", 0), ("xn2f", 1))
                        sb.wr(GP, ("xn2_sp", lt, u))
                        GP.mark(nc.gpsimd.dma_start(
                            xn2_sp.ap()[lt, :, :, u * 128:(u + 1) * 128],
                            xn2_fmu[:, :, :]))

            engs = [e.e for e in ALLE]
            pre = sb.state()
            posts = []
            for j in nc.Switch(engines=engs, index=[jreg] * len(engs), n=2):
                if j > 0:
                    posts.append(sb.state())
                    sb.restore(pre)
                emit_phase_b(j)
            posts.append(sb.state())
            assert posts[0][2] == posts[1][2], (
                f"switch bodies diverge: {posts[0][2]} vs {posts[1][2]}")
            for e in ALLE:
                (n0, w0, c0) = posts[0][3][e.name]
                (n1, w1, c1) = posts[1][3][e.name]
                assert n0 == n1, (e.name, n0, n1)
                if e.is_dma:
                    assert c0 == c1, (e.name, c0, c1)
                else:
                    e.last_sb = min(c0, c1)
                e.waited = {k: min(w0.get(k, 0), w1.get(k, 0))
                            for k in set(w0) | set(w1)}
            sb.barrier()

    if dump == "B":
        SY.drain()
        GP.drain()
        return nc

    with ExitStack() as pc:
        sbC = lambda n, s, d: pc.enter_context(nc.sbuf_tensor(n, s, d))
        psC = lambda n, s, d=F32: pc.enter_context(nc.psum_tensor(n, s, d))
        mixed = sbC("mixed", [128, 16, D], F32)
        xn2a = sbC("xn2a", [128, 8, 2048], BF16)
        ar1 = sbC("ar1", [128, 8, H2], BF16)
        ar2 = sbC("ar2", [128, 16, D], BF16)
        h1 = sbC("h1", [128, 16, 512], BF16)
        u8 = sbC("u8", [128, E], F32)
        csel = sbC("csel", [128, E], F32)
        rstat = sbC("rstat", [128, 8], F32)
        wbf = sbC("wbf", [128, E], BF16)
        w_all = sbC("w_all", [128, 16, E], F32)
        wT9 = sbC("wT9", [9, 2048], BF16)
        rwq_sb = sbC("rwq_sb", [128, 8, E], BF16)
        rb_row = sbC("rb_row", [1, E], BF16)
        sb1_col = sbC("sb1_col", [128, 16], F32)
        rb1_col = sbC("rb1_col", [128, E, 16], F32)
        rb2sb2 = sbC("rb2sb2", [9, D], BF16)
        xb2 = [sbC("xb2_%d" % i, [128, D], F32) for i in range(2)]
        outb = [sbC("outb_%d" % i, [128, D], F32) for i in range(2)]
        ps_h1 = psC("ps_h1", [128, 1024])
        ps_y2 = psC("ps_y2", [128, 1024])
        ps_r = psC("ps_r", [128, 512])
        ps_tw = psC("ps_tw", [128, 128], BF16)

        sb.wr(GP, "constsC")
        _n0 = GP.n
        GP.mark(nc.gpsimd.dma_start(rb2sb2[0:8, :], eb2_h[:, :]))
        GP.mark(nc.gpsimd.dma_start(
            rb2sb2[8:9, :], sb2_h.ap().rearrange("(a d) -> a d", a=1)))
        GP.mark(nc.gpsimd.dma_start(
            rb_row[:, :], rb_h.ap().rearrange("(a d) -> a d", a=1)))
        with nc.allow_non_contiguous_dma(reason="tiny col packs, once"):
            GP.mark(nc.gpsimd.dma_start(
                rwq_sb[:, :, :], rw_h.ap().rearrange("(c p) e -> p c e",
                                                     p=128)))
            GP.mark(nc.gpsimd.dma_start(
                sb1_col[:, :], sb1_h.ap().rearrange("(c p) -> p c", p=128)))
            for ee in range(E):
                GP.mark(nc.gpsimd.dma_start(
                    rb1_col[:, ee, :],
                    eb1_h.ap()[ee].rearrange("(c p) -> p c", p=128)))
        sb.bind_range(GP, _n0, "constsC")
        sb.wr(V, "wT9ones")
        V.mark(nc.vector.memset(wT9[:, :], 1.0))

        for lt in range(4):
            sb.rd(SY, ("xn2_sp", lt, 0), ("xn2_sp", lt, 1),
                  ("xn2_sp", lt, 2), ("xn2_sp", lt, 3))
            sb.wr(SY, ("xn2a", lt))
            SY.mark(nc.sync.dma_start(xn2a[:, :, lt * 512:(lt + 1) * 512],
                                      xn2_sp.ap()[lt, :, :, :]))

        for u16 in range(16):
            lt = u16 // 4
            sb.rd(T, ("xn2a", lt), "constsC", "ones")
            sb.wr(T, "ps_r")
            for c in range(8):
                nc.tensor.matmul(ps_r[:, 0:E],
                                 xn2a[:, c, u16 * 128:(u16 + 1) * 128],
                                 rwq_sb[:, c, :], start=(c == 0), stop=False)
            T.mark(nc.tensor.matmul(ps_r[:, 0:E], ones_bf[0:1, :],
                                    rb_row[0:1, :], start=False, stop=True))
            sb.rd(V, "ps_r")
            sb.wr(V, "rstat")
            V.mark(nc.vector.reduce_max(rstat[:, 0:1], ps_r[:, 0:E],
                                        axis=AX.X, negate=True))
            sb.rd(SC, "ps_r", "rstat")
            sb.wr(SC, "u8", "rsum")
            SC.mark(nc.scalar.activation(u8[:, :], ps_r[:, 0:E], AF.Exp,
                                         bias=rstat[:, 0:1],
                                         accum_out=rstat[:, 1:2]))
            sb.rd(V, "u8")
            sb.wr(V, "rm1")
            V.mark(nc.vector.reduce_max(rstat[:, 2:3], u8[:, :], axis=AX.X))
            V.selfbar()
            sb.rd(V, "u8", "rm1")
            sb.wr(V, "csel")
            V.mark(nc.vector.tensor_scalar(csel[:, :], u8[:, :],
                                           rstat[:, 2:3], None, ALU.is_ge))
            sb.rd(V, "csel")
            sb.wr(V, "u8")
            V.mark(nc.vector.tensor_sub(u8[:, :], u8[:, :], csel[:, :]))
            sb.rd(V, "u8")
            sb.wr(V, "rm2")
            V.mark(nc.vector.reduce_max(rstat[:, 3:4], u8[:, :], axis=AX.X))
            V.selfbar()
            sb.rd(V, "csel")
            sb.wr(V, "u8")
            V.mark(nc.vector.tensor_add(u8[:, :], u8[:, :], csel[:, :]))
            sb.rd(V, "u8", "rm2")
            sb.wr(V, "csel")
            V.mark(nc.vector.tensor_scalar(csel[:, :], u8[:, :],
                                           rstat[:, 3:4], None, ALU.is_ge))
            sb.rd(V, "rsum")
            sb.wr(V, "rrec")
            V.mark(nc.vector.reciprocal(rstat[:, 4:5], rstat[:, 1:2]))
            V.selfbar()
            sb.rd(V, "u8", "csel", "rrec")
            sb.wr(V, ("w_all", u16))
            nc.vector.tensor_mul(w_all[:, u16, :], u8[:, :], csel[:, :])
            V.mark(nc.vector.tensor_scalar(w_all[:, u16, :], w_all[:, u16, :],
                                           rstat[:, 4:5], None, ALU.mult))
            sb.rd(V, ("w_all", u16))
            sb.wr(V, "wbf")
            V.mark(nc.vector.tensor_copy(wbf[:, :], w_all[:, u16, :]))
            sb.rd(T, "wbf", "consts")
            sb.wr(T, "ps_tw")
            T.mark(nc.tensor.transpose(ps_tw[0:E, :], wbf[:, :], ident[:, :]))
            sb.rd(V, "ps_tw", "wT9ones")
            sb.wr(V, ("wT9", u16))
            V.mark(nc.vector.tensor_copy(wT9[0:E, u16 * 128:(u16 + 1) * 128],
                                         ps_tw[0:E, :]))
        V.selfbar()

        def load_weights(tag, e):
            if tag == "sh":
                w1src = sw1_h.ap().rearrange("(c p) h -> p c h", p=128)
                w2src = sw2_h.ap().rearrange("(c p) d -> p c d", p=128)
            else:
                w1src = ew1_h.ap()[e].rearrange("(c p) h -> p c h", p=128)
                w2src = ew2_h.ap()[e].rearrange("(c p) d -> p c d", p=128)
            sb.wr(GP, "ar1")
            GP.mark(nc.gpsimd.dma_start(ar1[:, :, :], w1src))
            sb.wr(GP, "ar2")
            GP.mark(nc.gpsimd.dma_start(ar2[:, :, :], w2src))

        def ffn_pass(tag, e):
            for tt in range(4):
                for oc in range(16):
                    slot = oc % 2
                    sb.rd(T, "ar1", ("xn2a", tt))
                    sb.wr(T, ("ps_h1", slot))
                    ins = None
                    for c in range(8):
                        ins = nc.tensor.matmul(
                            ps_h1[:, slot * 512:(slot + 1) * 512],
                            ar1[:, c, oc * 128:(oc + 1) * 128],
                            xn2a[:, c, tt * 512:(tt + 1) * 512],
                            start=(c == 0), stop=(c == 7))
                    T.mark(ins)
                    bcol = (sb1_col[:, oc:oc + 1] if tag == "sh"
                            else rb1_col[:, e, oc:oc + 1])
                    sb.rd(SC, ("ps_h1", slot), "constsC")
                    sb.wr(SC, ("h1", oc))
                    SC.mark(nc.scalar.activation(
                        h1[:, oc, :], ps_h1[:, slot * 512:(slot + 1) * 512],
                        AF.Gelu, bias=bcol))
                for ts in range(4):
                    u16 = tt * 4 + ts
                    for half in range(2):
                        slot = half
                        sb.rd(T, "ar2", *[("h1", oc) for oc in range(16)])
                        if tag == "sh":
                            sb.rd(T, ("wT9", u16), "constsC", "wT9ones")
                        sb.wr(T, ("ps_y2", slot))
                        ins = None
                        for c in range(16):
                            last = (c == 15) and tag != "sh"
                            ins = nc.tensor.matmul(
                                ps_y2[:, slot * 512:(slot + 1) * 512],
                                h1[:, c, ts * 128:(ts + 1) * 128],
                                ar2[:, c, half * 512:(half + 1) * 512],
                                start=(c == 0), stop=last)
                        if tag == "sh":
                            ins = nc.tensor.matmul(
                                ps_y2[:, slot * 512:(slot + 1) * 512],
                                wT9[:, u16 * 128:(u16 + 1) * 128],
                                rb2sb2[:, half * 512:(half + 1) * 512],
                                start=False, stop=True)
                        T.mark(ins)
                        sb.rd(V, ("ps_y2", slot))
                        sb.wr(V, ("mixed", u16, half))
                        dstm = mixed[:, u16, half * 512:(half + 1) * 512]
                        if tag == "sh":
                            V.mark(nc.vector.tensor_copy(
                                dstm, ps_y2[:, slot * 512:(slot + 1) * 512]))
                        else:
                            sb.rd(V, ("w_all", u16))
                            V.mark(nc.vector.scalar_tensor_tensor(
                                dstm, ps_y2[:, slot * 512:(slot + 1) * 512],
                                w_all[:, u16, e:e + 1], dstm,
                                ALU.mult, ALU.add))

        load_weights("sh", None)
        ffn_pass("sh", None)
        if dump == "C1":
            sb.barrier()
            SY.mark(nc.sync.dma_start(xn2a_d.ap()[:, :, :], xn2a[:, :, :]))
            SY.mark(nc.sync.dma_start(wall_d.ap()[:, :, :], w_all[:, :, :]))
            SY.mark(nc.sync.dma_start(mix_d.ap()[:, :, :], mixed[:, :, :]))
            SY.mark(nc.sync.dma_start(wt9_d.ap()[:, :], wT9[:, :]))
            SY.drain()
            GP.drain()
            return nc
        for e in range(E):
            load_weights(e, e)
            ffn_pass(e, e)

        for u16 in range(16):
            lt, u = u16 // 4, u16 % 4
            slot = u16 % 2
            sb.rd(SY, ("x2_sp", lt, u))
            sb.wr(SY, ("xb2", slot))
            SY.mark(nc.sync.dma_start(xb2[slot][:, :],
                                      x2_sp.ap()[lt, :, u, :]))
            sb.rd(V, ("xb2", slot), ("mixed", u16, 0), ("mixed", u16, 1))
            sb.wr(V, ("outb", slot))
            V.mark(nc.vector.tensor_add(outb[slot][:, :], xb2[slot][:, :],
                                        mixed[:, u16, :]))
            sb.rd(GP, ("outb", slot))
            sb.wr(GP, ("outd", u16))
            GP.mark(nc.gpsimd.dma_start(out_h.ap()[lt, :, u, :],
                                        outb[slot][:, :]))

    GP.drain()
    SY.drain()
    st_ctx.close()
    return nc


def _in_maps(inputs):
    x = np.ascontiguousarray(np.asarray(inputs["L0_attn_hidden_states"],
                                        np.float32))
    f32 = lambda k: np.ascontiguousarray(np.asarray(inputs[k], np.float32))
    maps = []
    for core in range(8):
        b, j = core // 2, core % 2
        maps.append({
            "xf": np.ascontiguousarray(x[b]),
            "xs": np.ascontiguousarray(
                np.stack([x[b, g * 512:(g + 1) * 512] for g in MINE[j]])),
            "jsel": np.full((1, 1), j, np.int32),
            "wqkv": f32("L0_attn_Wqkv"), "bqkv": f32("L0_attn_bqkv"),
            "wqa": f32("L0_attn_Wqa"), "bqa": f32("L0_attn_bqa"),
            "wka": f32("L0_attn_Wka"), "bka": f32("L0_attn_bka"),
            "wd": f32("L0_attn_Wd"), "bd": f32("L0_attn_bd"),
            "rw": f32("L0_ffn_router_w"), "rb": f32("L0_ffn_router_b"),
            "sw1": f32("L0_ffn_sw1"), "sb1": f32("L0_ffn_sb1"),
            "sw2": f32("L0_ffn_sw2"), "sb2": f32("L0_ffn_sb2"),
            "ew1": f32("L0_ffn_rw1"), "eb1": f32("L0_ffn_rb1"),
            "ew2": f32("L0_ffn_rw2"), "eb2": f32("L0_ffn_rb2"),
        })
    return maps


def kernel(**inputs):
    nc = build()
    maps = _in_maps(inputs)
    res = run_bass_kernel_spmd(nc, maps, core_ids=list(range(8)))
    out = np.zeros((4, S, D), np.asarray(res.results[0]["out"]).dtype)
    for core in range(8):
        b, j = core // 2, core % 2
        o = np.asarray(res.results[core]["out"])
        o = o.transpose(0, 2, 1, 3).reshape(4, 512, D)
        for lt, g in enumerate(MINE[j]):
            out[b, g * 512:(g + 1) * 512] = o[lt]
    return out
